# version 03
# baseline (speedup 1.0000x reference)
import numpy as np
from contextlib import ExitStack
import concourse.bass as bass
import concourse.mybir as mybir

F32 = mybir.dt.float32
BF16 = mybir.dt.bfloat16
I32 = mybir.dt.int32
AF = mybir.ActivationFunctionType
ALU = mybir.AluOpType
AX = mybir.AxisListType

D = 2048
S = 8192
SEG = 512
NSEG = 16
HPC = 4
DK = DV = 128
HID = 8192
EPS_LN = 1e-5
EPS_FFN = 1e-8
ATANH_HALF = 0.5493061443340549
INV_SC = float(1.0 / np.sqrt(128.0))
NW_ELEM = float(HID * D)

ENGINES = ("sync", "vector", "scalar", "tensor", "gpsimd")
ESEM = {"sync": "dx", "gpsimd": "po", "vector": "dv", "scalar": "ac", "tensor": "pe"}
DSEM = {"sync": "dx", "gpsimd": "dg", "scalar": "ds"}


def mmax(*marks):
    out = {}
    for m in marks:
        if not m:
            continue
        for s, v in m.items():
            out[s] = max(out.get(s, 0), v)
    return out


class Prog:
    def __init__(self):
        self.ops = []
        self.tot = {}
        self.buf = {}

    def set_barrier(self):
        self.barrier_pending = {e: dict(self.tot) for e in ENGINES}

    def emit(self, engine, fn, reads=(), sreads=(), writes=(), inc=(), tag=""):
        w_data = {}
        w_strict = {}
        bp = getattr(self, "barrier_pending", None)
        if bp and engine in bp:
            w_data = mmax(w_data, bp.pop(engine))
        for b in reads:
            st = self.buf.get(b)
            if st:
                w_data = mmax(w_data, st["prod"])
        for b in sreads:
            st = self.buf.get(b)
            if st:
                w_strict = mmax(w_strict, st["prod"])
        for b in writes:
            st = self.buf.get(b)
            if st:
                w_data = mmax(w_data, st["cons"], st["prod"])
        inc = tuple(inc)
        assert inc, f"op without inc: {tag}"
        for s, k in inc:
            self.tot[s] = self.tot.get(s, 0) + k
        mark = {s: self.tot[s] for s, _ in inc}
        self.ops.append((engine, fn, dict(w_data), dict(w_strict), inc, tag))
        for b in writes:
            self.buf[b] = {"prod": dict(mark), "cons": {}}
        for b in tuple(reads) + tuple(sreads):
            st = self.buf.get(b)
            if st is None:
                self.buf[b] = {"prod": {}, "cons": dict(mark)}
            else:
                st["cons"] = mmax(st["cons"], mark)
        return mark

    def now(self, sem):
        return self.tot.get(sem, 0)

    def truncate(self, n):
        if n is None or n >= len(self.ops):
            return
        self.ops = self.ops[:n]
        tot = {}
        for _, _, _, _, inc, _ in self.ops:
            for s, k in inc:
                tot[s] = tot.get(s, 0) + k
        self.tot = tot

    def finalize(self):
        used = {o[0] for o in self.ops}
        allw = {s: v for s, v in self.tot.items()}
        for eng_name in ENGINES:
            if eng_name in used:
                self.ops.append((eng_name, None, {}, dict(allw), (), "final_wait"))

    def build_streams(self, blk, sems):
        for eng_name in ENGINES:
            ops = [o for o in self.ops if o[0] == eng_name]
            if not ops:
                continue
            own = ESEM[eng_name]

            def mk(ops_, own_):
                def body(eng):
                    drop_own = own_ == "pe"
                    for _, fn, w_data, w_strict, inc, tag in ops_:
                        w = dict(w_strict)
                        for s, v in w_data.items():
                            if s == own_ and drop_own:
                                continue
                            w[s] = max(w.get(s, 0), v)
                        if fn is None:
                            w.pop(own_, None)
                        for s, v in sorted(w.items()):
                            if v > 0:
                                eng.wait_ge(sems[s], v)
                        if fn is None:
                            continue
                        inst = fn(eng)
                        for s, k in inc:
                            assert inst is not None, f"then_inc on None: {tag}"
                            inst = inst.then_inc(sems[s], k)
                return body
            getattr(blk, eng_name)(mk(ops, own))


def build_core_program(core_id, debug=0, nseg=NSEG, do_ap=True, do_b=True):
    nc = bass.Bass("TRN2", target_bir_lowering=False, detect_race_conditions=False)
    P = Prog()
    dbg = {}
    p_ar_ref = [None]
    cp = core_id % 4
    own0 = 128 * cp

    def E(engine, fn, reads=(), sreads=(), writes=(), k=None, tag=""):
        sem = ESEM[engine]
        if k is None:
            k = 16 if engine in ("sync", "gpsimd") and "dma" in tag else 1
        return P.emit(engine, fn, reads=reads, sreads=sreads, writes=writes,
                      inc=[(sem, k)], tag=tag)

    def DMA(engine, fn, reads=(), writes=(), tag=""):
        return P.emit(engine, fn, reads=reads, writes=writes,
                      inc=[(DSEM[engine], 16)], tag=tag)

    x = nc.dram_tensor("x", [S, D], F32, kind="ExternalInput").ap()
    xo_in = nc.dram_tensor("xo", [NSEG * 128, D], F32, kind="ExternalInput").ap()
    wq = nc.dram_tensor("wq", [D, HPC * DK], F32, kind="ExternalInput").ap()
    wk = nc.dram_tensor("wk", [D, HPC * DK], F32, kind="ExternalInput").ap()
    wv = nc.dram_tensor("wv", [D, HPC * DK], F32, kind="ExternalInput").ap()
    wo = nc.dram_tensor("wo", [D, D], F32, kind="ExternalInput").ap()
    betasT = nc.dram_tensor("betasT", [DV, HPC], F32, kind="ExternalInput").ap()
    w1t = nc.dram_tensor("w1t", [D, HID], F32, kind="ExternalInput").ap()
    w2t = nc.dram_tensor("w2t", [D, HID], F32, kind="ExternalInput").ap()
    w3t = nc.dram_tensor("w3t", [HID, D], F32, kind="ExternalInput").ap()
    g1 = nc.dram_tensor("g1", [1, D], F32, kind="ExternalInput").ap()
    g2 = nc.dram_tensor("g2", [1, D], F32, kind="ExternalInput").ap()
    g3 = nc.dram_tensor("g3", [1, HID], F32, kind="ExternalInput").ap()
    ident_in = nc.dram_tensor("ident", [128, 128], BF16, kind="ExternalInput").ap()
    dmask_in = nc.dram_tensor("dmask", [128, 128], F32, kind="ExternalInput").ap()
    meminit_in = nc.dram_tensor("meminit", [128, 130], F32, kind="ExternalInput").ap()
    cst_in = nc.dram_tensor("cst", [128, 3], F32, kind="ExternalInput").ap()

    out = nc.dram_tensor("out", [NSEG * 128, D], F32, kind="ExternalOutput").ap()

    attfm_scr = nc.dram_tensor("attfm_scr", [NSEG * HPC, 128, SEG], BF16, kind="Internal").ap()
    attown_scr = nc.dram_tensor("attown_scr", [NSEG * 128, D], F32, kind="Internal").ap()
    h_scr = nc.dram_tensor("h_scr", [NSEG * 128, HID], F32, kind="Internal").ap()
    sred_scr = nc.dram_tensor("sred_scr", [1, 384], F32, kind="Internal").ap()

    if debug:
        for nm, shp, dt in (("xnfm", [16, 128, SEG], BF16), ("qfm", [128, SEG], BF16),
                            ("sq", [128, SEG], BF16), ("vtm", [128, 4 * SEG], BF16),
                            ("p0", [128, 4 * SEG], BF16),
                            ("attfm", [NSEG * HPC, 128, SEG], BF16),
                            ("attown", [NSEG * 128, D], F32),
                            ("svals", [128, 12], F32),
                            ("n1", [128, D], BF16),
                            ("t1t", [128, 512], BF16),
                            ("hrow", [128, HID], F32),
                            ("o3", [128, D], F32),
                            ("swx", [128, 1024], F32),
                            ("probe", [128, 4096], F32)):
            dbg[nm] = nc.dram_tensor("dbg_" + nm, shp, dt, kind="ExternalOutput").ap()

    ctx = ExitStack()

    def sb(name, shape, dt):
        return nc.alloc_sbuf_tensor(name, shape, dt)

    ident = sb("c_ident", [128, 128], BF16)
    cst = sb("c_cst", [128, 3], F32)
    dmask = sb("c_dmask", [128, 128], F32)
    onesrow = sb("c_onesrow", [1, 128], BF16)
    beta_c = sb("c_beta", [128, HPC], F32)
    ombeta_c = sb("c_ombeta", [128, HPC], F32)
    scol = sb("c_scol", [128, 12], F32)
    spart = sb("c_spart", [128, 4], F32)
    stmp = sb("c_stmp", [128, 1], F32)
    srow384 = sb("c_srow384", [1, 384], F32)
    svrow = sb("c_svrow", [1, 16], F32)
    hl_hi = sb("c_hlhi", [1, 512], BF16)
    hl_lo = sb("c_hllo", [1, 512], BF16)
    hl_tmp = sb("c_hltmp", [1, 512], F32)

    ps = [nc.alloc_psum_tensor(f"ps{i}", [128, 512], F32) for i in range(6)]
    ptbA = nc.alloc_psum_tensor("ptbA", [128, 512], BF16)
    ptbB = nc.alloc_psum_tensor("ptbB", [128, 512], BF16)
    ptb = [ptbA[:], ptbB[:]]

    sems = {}
    for s_name in ("dx", "dg", "ds", "pe", "dv", "ac", "po", "fin"):
        sems[s_name] = ctx.enter_context(nc.semaphore("sem_" + s_name))

    def bcast_row(tag, row_buf, row_ap, n, pdst_name, pdst_ap):
        E("vector", lambda e: e.tensor_copy(out=hl_hi[:, 0:n], in_=row_ap),
          reads=(row_buf,), writes=("hl_hi",), tag=tag + "_hi")
        E("vector", lambda e: e.tensor_copy(out=hl_tmp[:, 0:n], in_=hl_hi[:, 0:n]),
          reads=("hl_hi",), writes=("hl_tmp",), tag=tag + "_hf")
        E("vector", lambda e: e.tensor_tensor(out=hl_lo[:, 0:n], in0=row_ap,
                                              in1=hl_tmp[:, 0:n], op=ALU.subtract),
          reads=(row_buf, "hl_tmp"), writes=("hl_lo",), tag=tag + "_lo")
        E("tensor", lambda e: e.matmul(out=pdst_ap, lhsT=onesrow[:], rhs=hl_hi[:, 0:n],
                                       start=True, stop=False),
          reads=("hl_hi", "onesrow"), writes=(pdst_name,), tag=tag + "_m1")
        return E("tensor", lambda e: e.matmul(out=pdst_ap, lhsT=onesrow[:], rhs=hl_lo[:, 0:n],
                                              start=False, stop=True),
                 reads=("hl_lo",), writes=(pdst_name,), tag=tag + "_m2")

    DMA("sync", lambda e: e.dma_start(out=ident[:], in_=ident_in), writes=("ident",), tag="dma_ident")
    DMA("sync", lambda e: e.dma_start(out=cst[:], in_=cst_in), writes=("cst",), tag="dma_cst")
    DMA("sync", lambda e: e.dma_start(out=dmask[:], in_=dmask_in), writes=("dmask",), tag="dma_dmask")
    DMA("sync", lambda e: e.dma_start(out=beta_c[:], in_=betasT), writes=("beta_raw",), tag="dma_beta")
    E("scalar", lambda e: e.activation(out=beta_c[:], in_=beta_c[:], func=AF.Sigmoid),
      reads=("beta_raw",), writes=("beta",), tag="sig_beta")
    E("scalar", lambda e: e.activation(out=ombeta_c[:], in_=beta_c[:], func=AF.Copy,
                                       bias=1.0, scale=-1.0),
      reads=("beta",), writes=("ombeta",), tag="ombeta")
    E("vector", lambda e: e.memset(onesrow[:], 1.0), writes=("onesrow",), tag="ones_r")
    E("vector", lambda e: e.memset(spart[:], 0.0), writes=("spart",), tag="spart0")
    if debug:
        P.emit("vector", lambda e: e.memset(p_ar_ref[0][:], 0.0), writes=("p0", "p1", "p2", "p3"),
               inc=[("dv", 1)], tag="p_ar0")

    sbuf_base_A = nc.sbuf_base

    def sba(name, shape, dt):
        return nc.alloc_sbuf_tensor(name, shape, dt)

    wq_sb = sba("a_wq", [128, 16 * 512], BF16)
    wk_sb = sba("a_wk", [128, 16 * 512], BF16)
    wv_sb = sba("a_wv", [128, 16 * 512], BF16)
    x4 = sba("a_x4", [128, 4 * D], F32)
    xn_tm = sba("a_xn_tm", [128, 4 * D], BF16)
    xn_fm = sba("a_xn_fm", [128, 16 * 512], BF16)
    v_bf = sba("a_v", [128, 4 * 512], BF16)
    q_bf = sba("a_q", [128, 512], BF16)
    k_bf = sba("a_k", [128, 512], BF16)
    sq_bf = sba("a_sq", [128, 512], BF16)
    sk_bf = sba("a_sk", [128, 512], BF16)
    sk_tm = sba("a_sktm", [128, 512], BF16)
    eq_f = sba("a_eq", [128, 512], F32)
    ek_f = sba("a_ek", [128, 512], F32)
    p_ar = sba("a_p", [128, 4 * 512], BF16)
    p_ar_ref[0] = p_ar
    pTa = sba("a_pTa", [128, 512], BF16)
    accs = sba("a_accs", [128, 4], F32)
    rl_bf = sba("a_rlbf", [128, 4], BF16)
    rl_f = sba("a_rlf", [128, 4], F32)
    rlrow_bf = sba("a_rlrow", [1, 512], BF16)
    rnrow_f = sba("a_rnrow", [1, 512], F32)
    mem_f = sba("a_memf", [128, 132 * HPC], F32)
    memz_bf = sba("a_memz", [128, 132], BF16)
    delta_bf = sba("a_delta", [128, 512], BF16)
    zups = sba("a_zups", [128, 1], F32)
    u_f = sba("a_u", [128, 512], F32)
    t2_f = sba("a_t2", [128, 512], F32)
    t3_f = sba("a_t3", [128, 512], F32)
    att_sb = sba("a_attbf", [128, 512], BF16)
    ln_st = sba("a_lnst", [128, 24], F32)
    ln_ag = sba("a_lnag", [128, 2], F32)
    sd_c = sba("a_sd", [128, 1], F32)
    rstd_c = sba("a_rstd", [128, 1], F32)
    lnb_c = sba("a_lnb", [128, 1], F32)
    rmax_c = sba("a_rmax", [128, 1], F32)
    ebias_c = sba("a_ebias", [128, 1], F32)
    rkn_c = sba("a_rkn", [128, 1], F32)
    wstg = [sba(f"a_wstg{i}", [128, 2048], F32) for i in range(2)]
    wabs = sba("a_wabs", [128, 2048], F32)

    def ldw(dst, dstname, src, tag):
        return DMA("gpsimd", lambda e: e.dma_start(
            out=dst.ap().rearrange("p (c n) -> p c n", c=16),
            in_=src.rearrange("(c p) n -> p c n", p=128)), writes=(dstname,), tag=tag)

    ldw(wq_sb, "wq_sb", wq, "dma_wq")
    ldw(wk_sb, "wk_sb", wk, "dma_wk")
    ldw(wv_sb, "wv_sb", wv, "dma_wv")
    for h in range(HPC):
        DMA("sync", lambda e, h=h: e.dma_start(out=mem_f[:, h * 132:h * 132 + 130],
                                               in_=meminit_in),
            writes=(f"mem{h}", f"mem{h}z"), tag=f"dma_mem{h}")

    s_tiles = []
    for wi, wsrc in enumerate((w1t, w2t, w3t)):
        rows, cols = wsrc.shape
        for r0 in range(0, rows, 128):
            for c0 in range(0, cols, 2048):
                s_tiles.append((wi, wsrc[r0:r0 + 128, c0:c0 + 2048]))
    s_i = [0]

    def emit_s_tiles(kk):
        for _ in range(kk):
            i = s_i[0]
            if i >= len(s_tiles):
                return
            s_i[0] += 1
            wi, src = s_tiles[i]
            bufn = f"wstg{i % 2}"
            buf = wstg[i % 2]
            DMA("scalar", lambda e, src=src, buf=buf: e.dma_start(out=buf[:], in_=src),
                writes=(bufn,), tag=f"dma_s{i}")
            E("scalar", lambda e, buf=buf: e.activation(out=wabs[:], in_=buf[:], func=AF.Abs,
                                                        accum_out=stmp[:]),
              reads=(bufn,), writes=("wabs", "stmp"), tag=f"sab{i}")
            E("vector", lambda e, wi=wi: e.tensor_tensor(
                out=spart[:, wi:wi + 1], in0=spart[:, wi:wi + 1], in1=stmp[:], op=ALU.add),
                reads=("stmp",), writes=("spart",), tag=f"sadd{i}")

    NS_PER_SEG = (len(s_tiles) + nseg - 1) // nseg

    for sg in range(nseg):
        xv = x.rearrange("(s j p) d -> s p j d", s=16, j=4, p=128)[sg]
        DMA("sync", lambda e, xv=xv: e.dma_start(
            out=x4.ap().rearrange("p (j d) -> p j d", j=4), in_=xv),
            writes=("x4",), tag=f"dma_x{sg}")

        for j in range(4):
            for kq in range(4):
                E("vector", lambda e, j=j, kq=kq: e.bn_stats(
                    out=ln_st[:, kq * 6:(kq + 1) * 6],
                    in_=x4[:, j * D + kq * 512: j * D + (kq + 1) * 512]),
                    reads=("x4",), writes=("ln_st",), tag=f"bn{sg}_{j}_{kq}")
            E("vector", lambda e: e.bn_aggr(out=ln_ag[:], in_=ln_st[:]),
              reads=("ln_st",), writes=("ln_ag",), tag=f"ag{sg}_{j}")
            E("scalar", lambda e: e.activation(out=sd_c[:], in_=ln_ag[:, 1:2],
                                               func=AF.Sqrt, bias=cst[:, 0:1]),
              reads=("ln_ag",), sreads=("cst",), writes=("sd",), tag=f"lsd{sg}_{j}")
            E("vector", lambda e: e.reciprocal(out=rstd_c[:], in_=sd_c[:]),
              reads=("sd",), writes=("rstd",), tag=f"rstd{sg}_{j}")
            E("vector", lambda e: e.scalar_tensor_tensor(
                out=lnb_c[:], in0=ln_ag[:, 0:1], scalar=-1.0, in1=rstd_c[:],
                op0=ALU.mult, op1=ALU.mult),
                reads=("ln_ag", "rstd"), writes=("lnb",), tag=f"lnb{sg}_{j}")
            E("vector", lambda e, j=j: e.tensor_scalar(
                out=xn_tm[:, j * D:(j + 1) * D], in0=x4[:, j * D:(j + 1) * D],
                scalar1=rstd_c[:], scalar2=lnb_c[:], op0=ALU.mult, op1=ALU.add),
                reads=("x4",), sreads=("lnb", "rstd"), writes=(f"xn_tm{j}",),
                tag=f"lna{sg}_{j}")

        for c in range(16):
            pb = c % 2
            pbn = f"ptb{pb}"
            for j in range(4):
                E("tensor", lambda e, c=c, j=j, pb=pb: e.transpose(
                    out=ptb[pb][:, j * 128:(j + 1) * 128],
                    in_=xn_tm[:, j * D + c * 128: j * D + (c + 1) * 128],
                    identity=ident[:]),
                    reads=(f"xn_tm{j}", "ident"), writes=(pbn,), tag=f"xnT{sg}_{c}_{j}")
            E("vector", lambda e, c=c, pb=pb: e.tensor_copy(
                out=xn_fm[:, c * 512:(c + 1) * 512], in_=ptb[pb][:]),
                reads=(pbn,), writes=(f"xn_fm{c}",), tag=f"xnTe{sg}_{c}")

        if debug and sg == 0:
            DMA("gpsimd", lambda e: e.dma_start(
                out=dbg["xnfm"].rearrange("c p n -> p c n"),
                in_=xn_fm.ap().rearrange("p (c n) -> p c n", c=16)),
                reads=tuple(f"xn_fm{c}" for c in range(16)), tag="dma_dbgxnfm")

        for tt in range(4):
            for c in range(16):
                E("tensor", lambda e, tt=tt, c=c: e.matmul(
                    out=ps[2][:], lhsT=xn_fm[:, c * 512 + tt * 128: c * 512 + (tt + 1) * 128],
                    rhs=wv_sb[:, c * 512:(c + 1) * 512],
                    start=(c == 0), stop=(c == 15)),
                    reads=(f"xn_fm{c}", "wv_sb"), writes=("ps2",), tag=f"vmm{sg}_{tt}_{c}")
            E("scalar", lambda e, tt=tt: e.activation(
                out=v_bf[:, tt * 512:(tt + 1) * 512], in_=ps[2][:], func=AF.Copy),
                reads=("ps2",), writes=(f"v{tt}",), tag=f"vev{sg}_{tt}")

        if debug and sg == 0:
            DMA("gpsimd", lambda e: e.dma_start(out=dbg["vtm"], in_=v_bf[:]),
                reads=("v0", "v1", "v2", "v3"), tag="dma_dbgv")

        emit_s_tiles(NS_PER_SEG // 2)

        for h in range(HPC):
            hof = h * 132
            for c in range(16):
                E("tensor", lambda e, c=c, h=h: e.matmul(
                    out=ps[0][:], lhsT=wq_sb[:, c * 512 + h * 128: c * 512 + (h + 1) * 128],
                    rhs=xn_fm[:, c * 512:(c + 1) * 512],
                    start=(c == 0), stop=(c == 15)),
                    reads=(f"xn_fm{c}", "wq_sb"), writes=("ps0",), tag=f"qmm{sg}_{h}_{c}")
            for c in range(16):
                E("tensor", lambda e, c=c, h=h: e.matmul(
                    out=ps[1][:], lhsT=wk_sb[:, c * 512 + h * 128: c * 512 + (h + 1) * 128],
                    rhs=xn_fm[:, c * 512:(c + 1) * 512],
                    start=(c == 0), stop=(c == 15)),
                    reads=(f"xn_fm{c}", "wk_sb"), writes=("ps1",), tag=f"kmm{sg}_{h}_{c}")
            E("scalar", lambda e: e.activation(out=q_bf[:], in_=ps[0][:], func=AF.Copy),
              reads=("ps0",), writes=("q_bf",), tag=f"qc{sg}_{h}")
            E("scalar", lambda e: e.activation(out=eq_f[:], in_=ps[0][:], func=AF.Exp),
              reads=("ps0",), writes=("eq",), tag=f"qe{sg}_{h}")
            E("vector", lambda e: e.tensor_single_scalar(out=eq_f[:], in_=eq_f[:],
                                                         scalar=1.0, op=ALU.min),
              reads=("eq",), writes=("eq",), tag=f"qm{sg}_{h}")
            E("vector", lambda e: e.scalar_tensor_tensor(
                out=sq_bf[:], in0=q_bf[:], scalar=0.0, in1=eq_f[:], op0=ALU.max, op1=ALU.add),
                reads=("q_bf", "eq"), writes=("sq_bf",), tag=f"sq{sg}_{h}")
            E("scalar", lambda e: e.activation(out=k_bf[:], in_=ps[1][:], func=AF.Copy),
              reads=("ps1",), writes=("k_bf",), tag=f"kc{sg}_{h}")
            E("scalar", lambda e: e.activation(out=ek_f[:], in_=ps[1][:], func=AF.Exp),
              reads=("ps1",), writes=("ek",), tag=f"ke{sg}_{h}")
            E("vector", lambda e: e.tensor_single_scalar(out=ek_f[:], in_=ek_f[:],
                                                         scalar=1.0, op=ALU.min),
              reads=("ek",), writes=("ek",), tag=f"km{sg}_{h}")
            E("vector", lambda e: e.scalar_tensor_tensor(
                out=sk_bf[:], in0=k_bf[:], scalar=0.0, in1=ek_f[:], op0=ALU.max, op1=ALU.add),
                reads=("k_bf", "ek"), writes=("sk_bf",), tag=f"sk{sg}_{h}")

            if debug and sg == 0 and h == 0:
                DMA("gpsimd", lambda e: e.dma_start(out=dbg["qfm"], in_=q_bf[:]),
                    reads=("q_bf",), tag="dma_dbgq")
                DMA("gpsimd", lambda e: e.dma_start(out=dbg["sq"], in_=sq_bf[:]),
                    reads=("sq_bf",), tag="dma_dbgsq")

            for i in range(4):
                W = (i + 1) * 128
                pscn = "ps2" if i % 2 == 0 else "ps3"
                psc = ps[2 + (i % 2)]
                E("tensor", lambda e, psc=psc, i=i, W=W: e.matmul(
                    out=psc[:, 0:W], lhsT=q_bf[:, i * 128:(i + 1) * 128], rhs=k_bf[:, 0:W],
                    start=True, stop=True),
                    reads=("q_bf", "k_bf"), writes=(pscn,), tag=f"scm{sg}_{h}_{i}")
                E("vector", lambda e, psc=psc, i=i: e.tensor_tensor(
                    out=psc[:, i * 128:(i + 1) * 128], in0=psc[:, i * 128:(i + 1) * 128],
                    in1=dmask[:], op=ALU.add),
                    reads=(pscn, "dmask"), writes=(pscn,), tag=f"msk{sg}_{h}_{i}")
                E("vector", lambda e, psc=psc, W=W: e.reduce_max(
                    out=rmax_c[:], in_=psc[:, 0:W], axis=AX.X),
                    reads=(pscn,), writes=("rmax",), tag=f"rmx{sg}_{h}_{i}")
                E("vector", lambda e: e.tensor_single_scalar(
                    out=ebias_c[:], in_=rmax_c[:], scalar=-INV_SC, op=ALU.mult),
                    reads=("rmax",), writes=("ebias",), tag=f"ebi{sg}_{h}_{i}")
                E("scalar", lambda e, psc=psc, i=i, W=W: e.activation(
                    out=p_ar[:, i * 512: i * 512 + W], in_=psc[:, 0:W], func=AF.Exp,
                    bias=ebias_c[:], scale=INV_SC, accum_out=accs[:, i:i + 1]),
                    reads=(pscn,), sreads=("ebias",), writes=(f"p{i}", f"acc{i}"),
                    tag=f"exp{sg}_{h}_{i}")
            E("vector", lambda e: e.reciprocal(out=rl_f[:], in_=accs[:]),
              reads=("acc0", "acc1", "acc2", "acc3"), writes=("rl_f",), tag=f"rl{sg}_{h}")
            E("vector", lambda e: e.tensor_copy(out=rl_bf[:], in_=rl_f[:]),
              reads=("rl_f",), writes=("rl_bf",), tag=f"rlc{sg}_{h}")
            for i in range(4):
                E("tensor", lambda e, i=i: e.transpose(
                    out=ptb[0][0:1, i * 128:(i + 1) * 128], in_=rl_bf[:, i:i + 1],
                    identity=ident[:]),
                    reads=("rl_bf",), writes=("ptb0",), tag=f"rlT{sg}_{h}_{i}")
            E("vector", lambda e: e.tensor_copy(out=rlrow_bf[:], in_=ptb[0][0:1, 0:512]),
              reads=("ptb0",), writes=("rlrow",), tag=f"rle{sg}_{h}")

            if debug and sg == 0 and h == 0:
                DMA("gpsimd", lambda e: e.dma_start(out=dbg["p0"], in_=p_ar[:]),
                    reads=("p0", "p1", "p2", "p3"), tag="dma_dbgp")

            for j in range(4):
                pb = j % 2
                pbn = f"ptb{pb}"
                for i in range(j, 4):
                    E("tensor", lambda e, i=i, j=j, pb=pb: e.transpose(
                        out=ptb[pb][:, i * 128:(i + 1) * 128],
                        in_=p_ar[:, i * 512 + j * 128: i * 512 + (j + 1) * 128],
                        identity=ident[:]),
                        reads=(f"p{i}",), writes=(pbn,), tag=f"pT{sg}_{h}_{j}_{i}")
                E("vector", lambda e, j=j, pb=pb: e.tensor_copy(
                    out=pTa[:, j * 128:512], in_=ptb[pb][:, j * 128:512]),
                    reads=(pbn,), writes=("pTa",), tag=f"pTe{sg}_{h}_{j}")
                E("tensor", lambda e, j=j, h=h: e.matmul(
                    out=ps[4][:, j * 128:512],
                    lhsT=v_bf[:, j * 512 + h * 128: j * 512 + (h + 1) * 128],
                    rhs=pTa[:, j * 128:512], start=(j == 0), stop=(j == 3)),
                    reads=("pTa", f"v{j}"), writes=("ps4",), tag=f"adm{sg}_{h}_{j}")

            E("scalar", lambda e, hof=hof: e.activation(
                out=memz_bf[:, 0:130], in_=mem_f[:, hof:hof + 130], func=AF.Copy),
                reads=(f"mem{h}", f"mem{h}z"), writes=("memz",), tag=f"mzc{sg}_{h}")
            E("tensor", lambda e: e.matmul(
                out=ps[3][:], lhsT=memz_bf[:, 0:128], rhs=sq_bf[:], start=True, stop=True),
                reads=("memz", "sq_bf"), writes=("ps3",), tag=f"amm{sg}_{h}")
            E("tensor", lambda e: e.matmul(
                out=ps[5][0:1, :], lhsT=memz_bf[:, 128:129], rhs=sq_bf[:], start=True, stop=True),
                reads=("memz", "sq_bf"), writes=("ps5",), tag=f"zrm{sg}_{h}")
            E("vector", lambda e: e.reciprocal(out=rnrow_f[:], in_=ps[5][0:1, :]),
              reads=("ps5",), writes=("rnrow_f",), tag=f"rnr{sg}_{h}")

            for cch in range(4):
                E("tensor", lambda e, cch=cch: e.matmul(
                    out=ps[5][:, 0:130], lhsT=sk_bf[:, cch * 128:(cch + 1) * 128],
                    rhs=memz_bf[:, 0:130], start=True, stop=True),
                    reads=("sk_bf", "memz"), writes=("ps5",), tag=f"kdm{sg}_{h}_{cch}")
                E("vector", lambda e: e.reciprocal(out=rkn_c[:], in_=ps[5][:, 128:129]),
                  reads=("ps5",), writes=("rkn",), tag=f"kdr{sg}_{h}_{cch}")
                E("vector", lambda e, cch=cch, h=h: e.scalar_tensor_tensor(
                    out=delta_bf[:, cch * 128:(cch + 1) * 128], in0=ps[5][:, 0:128],
                    scalar=rkn_c[:], in1=v_bf[:, cch * 512 + h * 128:cch * 512 + (h + 1) * 128],
                    op0=ALU.mult, op1=ALU.subtract),
                    reads=("ps5", f"v{cch}"), sreads=("rkn",), writes=(f"delta{cch}",),
                    tag=f"dlt{sg}_{h}_{cch}")
                pb = cch % 2
                E("tensor", lambda e, cch=cch, pb=pb: e.transpose(
                    out=ptb[pb][:, cch * 128:(cch + 1) * 128],
                    in_=sk_bf[:, cch * 128:(cch + 1) * 128], identity=ident[:]),
                    reads=("sk_bf",), writes=(f"ptb{pb}",), tag=f"skT{sg}_{h}_{cch}")
                E("vector", lambda e, cch=cch, pb=pb: e.tensor_copy(
                    out=sk_tm[:, cch * 128:(cch + 1) * 128],
                    in_=ptb[pb][:, cch * 128:(cch + 1) * 128]),
                    reads=(f"ptb{pb}",), writes=(f"sktm{cch}",), tag=f"skTe{sg}_{h}_{cch}")
            for cch in range(4):
                E("tensor", lambda e, cch=cch: e.matmul(
                    out=ps[5][:, 0:128], lhsT=sk_tm[:, cch * 128:(cch + 1) * 128],
                    rhs=delta_bf[:, cch * 128:(cch + 1) * 128],
                    start=(cch == 0), stop=(cch == 3)),
                    reads=(f"sktm{cch}", f"delta{cch}"), writes=("ps5",), tag=f"mum{sg}_{h}_{cch}")
            E("vector", lambda e: e.reduce_sum(out=zups[:], in_=sk_bf[:], axis=AX.X),
              reads=("sk_bf",), writes=("zups",), tag=f"zsum{sg}_{h}")
            E("vector", lambda e, hof=hof: e.tensor_tensor(
                out=mem_f[:, hof:hof + 128], in0=mem_f[:, hof:hof + 128],
                in1=ps[5][:, 0:128], op=ALU.subtract),
                reads=("ps5",), writes=(f"mem{h}",), tag=f"mup{sg}_{h}")
            E("vector", lambda e, hof=hof: e.tensor_tensor(
                out=mem_f[:, hof + 128:hof + 129], in0=mem_f[:, hof + 128:hof + 129],
                in1=zups[:], op=ALU.add),
                reads=("zups",), writes=(f"mem{h}z",), tag=f"zup{sg}_{h}")

            bcast_row(f"rnb{sg}_{h}", "rnrow_f", rnrow_f[:], 512, "ps0", ps[0][:])
            bcast_row(f"rlb{sg}_{h}", "rlrow", rlrow_bf[:], 512, "ps1", ps[1][:])
            E("vector", lambda e, h=h: e.tensor_scalar(
                out=u_f[:], in0=ps[3][:], scalar1=beta_c[:, h:h + 1], scalar2=None,
                op0=ALU.mult),
                reads=("ps3",), sreads=("beta",), writes=("u_f",), tag=f"ep1{sg}_{h}")
            E("vector", lambda e: e.tensor_tensor(out=t2_f[:], in0=u_f[:], in1=ps[0][:],
                                                  op=ALU.mult),
              reads=("u_f", "ps0"), writes=("t2_f",), tag=f"ep2{sg}_{h}")
            E("vector", lambda e, h=h: e.tensor_scalar(
                out=u_f[:], in0=ps[4][:], scalar1=ombeta_c[:, h:h + 1], scalar2=None,
                op0=ALU.mult),
                reads=("ps4",), sreads=("ombeta",), writes=("u_f",), tag=f"ep3{sg}_{h}")
            E("vector", lambda e: e.tensor_tensor(out=t3_f[:], in0=u_f[:], in1=ps[1][:],
                                                  op=ALU.mult),
              reads=("u_f", "ps1"), writes=("t3_f",), tag=f"ep4{sg}_{h}")
            E("vector", lambda e: e.tensor_tensor(out=att_sb[:], in0=t2_f[:], in1=t3_f[:],
                                                  op=ALU.add),
              reads=("t2_f", "t3_f"), writes=("att_sb",), tag=f"ep5{sg}_{h}")
            DMA("gpsimd", lambda e, sg=sg, h=h: e.dma_start(
                out=attfm_scr[sg * HPC + h], in_=att_sb[:]),
                reads=("att_sb",), writes=(f"afm{sg}_{h}",), tag=f"dma_ast{sg}_{h}")
            if debug:
                DMA("gpsimd", lambda e, sg=sg, h=h: e.dma_start(
                    out=dbg["attfm"][sg * HPC + h], in_=att_sb[:]),
                    reads=("att_sb",), tag=f"dma_dbgast{sg}_{h}")

        emit_s_tiles(NS_PER_SEG - NS_PER_SEG // 2)

    emit_s_tiles(len(s_tiles))

    for wi in range(3):
        DMA("gpsimd", lambda e, wi=wi: e.dma_start(
            out=sred_scr[0:1, wi * 128:(wi + 1) * 128].rearrange("one n -> n one"),
            in_=spart[:, wi:wi + 1]),
            reads=("spart",), writes=(f"sred{wi}",), tag=f"dma_sw{wi}")
    DMA("sync", lambda e: e.dma_start(out=srow384[:], in_=sred_scr),
        reads=("sred0", "sred1", "sred2"), writes=("srow384",), tag="dma_sr")
    for wi in range(3):
        E("vector", lambda e, wi=wi: e.reduce_sum(
            out=hl_tmp[0:1, 8 * wi:8 * wi + 8],
            in_=srow384[0:1, wi * 128:(wi + 1) * 128].rearrange("one (a b) -> one a b", a=8),
            axis=AX.X),
            reads=("srow384",), writes=(f"svp{wi}",), tag=f"sredA{wi}")
        E("vector", lambda e, wi=wi: e.reduce_sum(
            out=svrow[0:1, 9 + wi:10 + wi], in_=hl_tmp[0:1, 8 * wi:8 * wi + 8],
            axis=AX.X),
            reads=(f"svp{wi}",), writes=("sv_s",), tag=f"sredB{wi}")
    E("vector", lambda e: e.tensor_single_scalar(
        out=svrow[0:1, 9:12], in_=svrow[0:1, 9:12], scalar=1.0 / NW_ELEM, op=ALU.mult),
        reads=("sv_s",), writes=("sv_s",), tag="snorm")
    E("vector", lambda e: e.tensor_scalar(
        out=svrow[0:1, 0:3], in0=svrow[0:1, 9:12], scalar1=EPS_FFN, scalar2=ATANH_HALF,
        op0=ALU.add, op1=ALU.mult),
        reads=("sv_s",), writes=("sv_th",), tag="sth")
    E("vector", lambda e: e.tensor_single_scalar(
        out=svrow[0:1, 3:6], in_=svrow[0:1, 0:3], scalar=-1.0, op=ALU.mult),
        reads=("sv_th",), writes=("sv_nth",), tag="snth")
    E("scalar", lambda e: e.activation(out=hl_tmp[0:1, 0:3], in_=svrow[0:1, 9:12],
                                       func=AF.Ln, bias=cst[0:1, 1:2], scale=1.0),
      reads=("sv_s",), sreads=("cst",), writes=("ln1p",), tag="sln1p")
    E("scalar", lambda e: e.activation(out=hl_tmp[0:1, 4:7], in_=svrow[0:1, 9:12],
                                       func=AF.Ln, bias=cst[0:1, 1:2], scale=-1.0),
      reads=("sv_s",), sreads=("cst",), writes=("ln1m",), tag="sln1m")
    E("vector", lambda e: e.tensor_tensor(out=svrow[0:1, 6:9], in0=hl_tmp[0:1, 0:3],
                                          in1=hl_tmp[0:1, 4:7], op=ALU.subtract),
      reads=("ln1p", "ln1m"), writes=("sv_a",), tag="sa1")
    E("vector", lambda e: e.tensor_single_scalar(
        out=svrow[0:1, 6:9], in_=svrow[0:1, 6:9], scalar=0.5, op=ALU.mult),
        reads=("sv_a",), writes=("sv_a",), tag="sa2")
    bcast_row("sbc", "sv_a", svrow[0:1, 0:12], 12, "ps5", ps[5][:, 0:12])
    E("vector", lambda e: e.tensor_copy(out=scol[:], in_=ps[5][:, 0:12]),
      reads=("ps5",), writes=("scol",), tag="scole")
    if debug:
        DMA("gpsimd", lambda e: e.dma_start(out=dbg["svals"], in_=scol[:]),
            reads=("scol",), tag="dma_dbgsv")

    nc.sbuf_base = sbuf_base_A

    final_marks = []

    if do_ap:
        P.set_barrier()
        sbuf_base_P = nc.sbuf_base

        def sbp(name, shape, dt):
            return nc.alloc_sbuf_tensor(name, shape, dt)

        wo_sb = sbp("p_wo", [128, 16 * 2048], BF16)
        att4 = sbp("p_att4", [128, 2 * 2048], BF16)
        ostage = sbp("p_ostage", [128, 2 * 2048], F32)

        DMA("gpsimd", lambda e: e.dma_start(
            out=wo_sb.ap().rearrange("p (c n) -> p c n", c=16),
            in_=wo.rearrange("(c p) n -> p c n", p=128)), writes=("wo_sb",), tag="dma_wo")

        for sg in range(nseg):
            db = sg % 2
            a4 = att4[:, db * 2048:(db + 1) * 2048]
            ost = ostage[:, db * 2048:(db + 1) * 2048]
            DMA("sync", lambda e, sg=sg, a4=a4: e.dma_start(
                out=a4.rearrange("p (h n) -> p h n", h=4),
                in_=attfm_scr[sg * HPC:(sg + 1) * HPC].rearrange("h p n -> p h n")),
                reads=tuple(f"afm{sg}_{h}" for h in range(HPC)),
                writes=(f"att4_{db}",), tag=f"dma_a4_{sg}")
            av = a4.rearrange("p (g m j) -> p g m j", g=4, m=32, j=16)
            for j in range(16):
                for fg in range(4):
                    E("tensor", lambda e, j=j, fg=fg, av=av: e.matmul(
                        out=ps[fg][:], lhsT=av[:, :, :, j],
                        rhs=wo_sb[:, j * 2048 + fg * 512: j * 2048 + (fg + 1) * 512],
                        start=(j == 0), stop=(j == 15)),
                        reads=(f"att4_{db}", "wo_sb"), writes=(f"ps{fg}",),
                        tag=f"wom{sg}_{j}_{fg}")
            for fg in range(4):
                E("scalar", lambda e, fg=fg, ost=ost: e.activation(
                    out=ost[:, fg * 512:(fg + 1) * 512], in_=ps[fg][:], func=AF.Copy),
                    reads=(f"ps{fg}",), writes=(f"ost{db}_{fg}",), tag=f"woe{sg}_{fg}")
            mow = DMA("gpsimd", lambda e, sg=sg, ost=ost: e.dma_start(
                out=attown_scr[sg * 128:(sg + 1) * 128], in_=ost),
                reads=tuple(f"ost{db}_{fg}" for fg in range(4)),
                writes=(f"aow{sg}",), tag=f"dma_ao_{sg}")
            if debug:
                DMA("gpsimd", lambda e, sg=sg, ost=ost: e.dma_start(
                    out=dbg["attown"][sg * 128:(sg + 1) * 128], in_=ost),
                    reads=tuple(f"ost{db}_{fg}" for fg in range(4)), tag=f"dma_dbgao_{sg}")
        nc.sbuf_base = sbuf_base_P

    if do_b:
        P.set_barrier()
        g1row = nc.alloc_sbuf_tensor("b_g1row", [1, D], F32)
        g2row = nc.alloc_sbuf_tensor("b_g2row", [1, D], F32)
        g3rsl = nc.alloc_sbuf_tensor("b_g3rsl", [1, 512], F32)
        g3b = nc.alloc_sbuf_tensor("b_g3b", [128, 512], F32)
        rs_ar = nc.alloc_sbuf_tensor("b_rs", [128, 48], F32)
        c3_ar = nc.alloc_sbuf_tensor("b_c3", [128, 16], F32)
        rstd3_ar = nc.alloc_sbuf_tensor("b_rstd3", [128, 16], F32)
        ssq3_ar = nc.alloc_sbuf_tensor("b_ssq3", [128, 16], F32)
        mxy3_ar = nc.alloc_sbuf_tensor("b_mxy3", [128, 16], F32)
        t1c = nc.alloc_sbuf_tensor("b_t1c", [128, 1], F32)
        t2c = nc.alloc_sbuf_tensor("b_t2c", [128, 1], F32)
        t3c = nc.alloc_sbuf_tensor("b_t3c", [128, 1], F32)
        t4c = nc.alloc_sbuf_tensor("b_t4c", [128, 1], F32)
        ln_st_b = nc.alloc_sbuf_tensor("b_lnst", [128, 24], F32)
        ln_ag_b = nc.alloc_sbuf_tensor("b_lnag", [128, 2], F32)

        DMA("sync", lambda e: e.dma_start(out=g1row[:], in_=g1), writes=("g1row",), tag="dma_g1")
        DMA("sync", lambda e: e.dma_start(out=g2row[:], in_=g2), writes=("g2row",), tag="dma_g2")
        E("vector", lambda e: e.memset(ssq3_ar[:], 0.0), writes=("ssq3",), tag="ssq30")
        E("vector", lambda e: e.memset(mxy3_ar[:], 0.0), writes=("mxy3",), tag="mxy30")
        E("vector", lambda e: e.memset(rs_ar[:], 0.0), writes=("rs_init",), tag="rs0")
        E("vector", lambda e: e.memset(c3_ar[:], 0.0), writes=("m3",), tag="c30")
        E("vector", lambda e: e.memset(rstd3_ar[:], 0.0), writes=("ms3",), tag="rstd30")

        base_B = nc.sbuf_base
        n1fm = nc.alloc_sbuf_tensor("b_n1fm", [128, 16 * 1024], BF16)
        n2fm = nc.alloc_sbuf_tensor("b_n2fm", [128, 16 * 1024], BF16)
        nc.sbuf_base = base_B
        o3acc = nc.alloc_sbuf_tensor("b_o3acc", [128, 8 * D], F32)
        base_stage = nc.sbuf_base

        xn_own = nc.alloc_sbuf_tensor("bp_xn", [128, D], F32)
        y_f = nc.alloc_sbuf_tensor("bp_y", [128, D], F32)
        tq_f = nc.alloc_sbuf_tensor("bp_t", [128, 1024], F32)
        ti_q = nc.alloc_sbuf_tensor("bp_ti", [128, 1024], I32)
        tf_q = nc.alloc_sbuf_tensor("bp_tf", [128, 1024], F32)
        w01_q = nc.alloc_sbuf_tensor("bp_w01", [128, 1024], F32)
        n_bf = nc.alloc_sbuf_tensor("bp_nbf", [128, D], BF16)
        g1b = nc.alloc_sbuf_tensor("bp_g1b", [128, D], F32)
        g2b = nc.alloc_sbuf_tensor("bp_g2b", [128, D], F32)
        prep_end = nc.sbuf_base
        nc.sbuf_base = base_stage
        T1t = nc.alloc_sbuf_tensor("bs_T1t", [128, 16 * 512], BF16)
        T2t = nc.alloc_sbuf_tensor("bs_T2t", [128, 16 * 512], BF16)
        wstg_b = nc.alloc_sbuf_tensor("bs_wstg", [128, 4 * 512], F32)
        sgA = nc.alloc_sbuf_tensor("bs_sgA", [128, 4 * 512], BF16)
        sgB = nc.alloc_sbuf_tensor("bs_sgB", [128, 4 * 512], BF16)
        sw_f = nc.alloc_sbuf_tensor("bs_sw", [128, 512], F32)
        xv_f = nc.alloc_sbuf_tensor("bs_xv", [128, 512], F32)
        h_f = nc.alloc_sbuf_tensor("bs_h", [128, 512], F32)
        y3_f = nc.alloc_sbuf_tensor("bs_y3", [128, 2 * 512], F32)
        sqd_f = nc.alloc_sbuf_tensor("bs_sqd", [128, 512], F32)
        slab_end = nc.sbuf_base
        nc.sbuf_base = base_stage
        T3t = nc.alloc_sbuf_tensor("b2_T3t", [128, 8 * D], BF16)
        w3stg = nc.alloc_sbuf_tensor("b2_wstg", [128, D], F32)
        sg3A = nc.alloc_sbuf_tensor("b2_sgA", [128, D], BF16)
        sg3B = nc.alloc_sbuf_tensor("b2_sgB", [128, D], BF16)
        y3t = nc.alloc_sbuf_tensor("b2_y3t", [128, 2 * 1024], F32)
        t3q_f = nc.alloc_sbuf_tensor("b2_t3", [128, 1024], F32)
        ti3 = nc.alloc_sbuf_tensor("b2_ti3", [128, 1024], I32)
        tf3 = nc.alloc_sbuf_tensor("b2_tf3", [128, 1024], F32)
        w013 = nc.alloc_sbuf_tensor("b2_w013", [128, 1024], F32)
        n3_bf = nc.alloc_sbuf_tensor("b2_n3", [128, 1024], BF16)
        n3fm = nc.alloc_sbuf_tensor("b2_n3fm", [128, 8 * 128], BF16)
        b2_end = nc.sbuf_base
        nc.sbuf_base = base_stage
        xo_t = nc.alloc_sbuf_tensor("bf_xo", [128, 2 * D], F32)
        ao_t = nc.alloc_sbuf_tensor("bf_ao", [128, 2 * D], F32)
        os_t = nc.alloc_sbuf_tensor("bf_os", [128, 2 * D], F32)
        nc.sbuf_base = max(prep_end, slab_end, b2_end)

        def quant_chain(src_ap, src_buf, cc_ap, cc_buf, dst_ap, dst_buf, width,
                        tbufs, tag):
            t_a, ti_a, tf_a, w_a = tbufs
            E("vector", lambda e: e.tensor_scalar(
                out=t_a[:, 0:width], in0=src_ap, scalar1=cc_ap, scalar2=128.5,
                op0=ALU.mult, op1=ALU.add),
                reads=(src_buf,), sreads=(cc_buf,), writes=(tag + "_t",), tag=tag + "_t")
            E("vector", lambda e: e.tensor_copy(out=ti_a[:, 0:width], in_=t_a[:, 0:width]),
              reads=(tag + "_t",), writes=(tag + "_ti",), tag=tag + "_ti")
            E("vector", lambda e: e.tensor_copy(out=tf_a[:, 0:width], in_=ti_a[:, 0:width]),
              reads=(tag + "_ti",), writes=(tag + "_tf",), tag=tag + "_tf")
            E("vector", lambda e: e.tensor_tensor(
                out=w_a[:, 0:width], in0=tf_a[:, 0:width], in1=t_a[:, 0:width], op=ALU.is_gt),
                reads=(tag + "_tf", tag + "_t"), writes=(tag + "_w",), tag=tag + "_w")
            E("vector", lambda e: e.scalar_tensor_tensor(
                out=dst_ap, in0=tf_a[:, 0:width], scalar=-128.0, in1=w_a[:, 0:width],
                op0=ALU.add, op1=ALU.subtract),
                reads=(tag + "_tf", tag + "_w"), writes=(dst_buf,), tag=tag + "_n")

        def ternary_tiles(wix, stg, stg_buf, stg_view, a_flat, b_flat, dst_ap, dst_buf,
                          src_sel, width, tag):
            DMA("scalar", lambda e: e.dma_start(out=stg_view, in_=src_sel),
                writes=(stg_buf,), tag=tag + "_ld")
            E("scalar", lambda e: e.activation(
                out=a_flat, in_=stg.ap()[:, 0:width], func=AF.Sign, bias=scol[:, 3 + wix:4 + wix]),
                reads=(stg_buf,), sreads=("scol",), writes=("sgA",), tag=tag + "_a")
            E("scalar", lambda e: e.activation(
                out=b_flat, in_=stg.ap()[:, 0:width], func=AF.Sign, bias=scol[:, 0 + wix:1 + wix]),
                reads=(stg_buf,), sreads=("scol",), writes=("sgB",), tag=tag + "_b")
            E("vector", lambda e: e.tensor_tensor(out=dst_ap, in0=a_flat, in1=b_flat, op=ALU.add),
              reads=("sgA", "sgB"), writes=(dst_buf,), tag=tag + "_s")

        for tg in range(2):
            P.set_barrier()
            for gi, (grow_ap, grow_buf, gb, gbn) in enumerate(
                    ((g1row, "g1row", g1b, "g1b"), (g2row, "g2row", g2b, "g2b"))):
                for q in range(4):
                    bcast_row(f"gb{tg}_{gi}_{q}", grow_buf, grow_ap[0:1, q * 512:(q + 1) * 512],
                              512, "ps4", ps[4][:])
                    E("vector", lambda e, gb=gb, q=q: e.tensor_copy(
                        out=gb[:, q * 512:(q + 1) * 512], in_=ps[4][:]),
                        reads=("ps4",), writes=(gbn + str(q),), tag=f"gbe{tg}_{gi}_{q}")
            for tt in range(8):
                gti = tg * 8 + tt
                DMA("sync", lambda e, gti=gti: e.dma_start(
                    out=xn_own[:], in_=xo_in[gti * 128:(gti + 1) * 128, :]),
                    writes=("xn_own_raw", "xn_own"), tag=f"dma_xo{gti}")
                for kq in range(4):
                    E("vector", lambda e, kq=kq: e.bn_stats(
                        out=ln_st_b[:, kq * 6:(kq + 1) * 6],
                        in_=xn_own[:, kq * 512:(kq + 1) * 512]),
                        reads=("xn_own_raw",), writes=("lnstb",), tag=f"bnb{gti}_{kq}")
                E("vector", lambda e: e.bn_aggr(out=ln_ag_b[:], in_=ln_st_b[:]),
                  reads=("lnstb",), writes=("lnagb",), tag=f"agb{gti}")
                E("scalar", lambda e: e.activation(out=t1c[:], in_=ln_ag_b[:, 1:2],
                                                   func=AF.Sqrt, bias=cst[:, 0:1]),
                  reads=("lnagb",), sreads=("cst",), writes=("t1c",), tag=f"sdb{gti}")
                E("vector", lambda e: e.reciprocal(out=t2c[:], in_=t1c[:]),
                  reads=("t1c",), writes=("rstdb",), tag=f"rstdb{gti}")
                E("vector", lambda e: e.scalar_tensor_tensor(
                    out=t3c[:], in0=ln_ag_b[:, 0:1], scalar=-1.0, in1=t2c[:],
                    op0=ALU.mult, op1=ALU.mult),
                    reads=("lnagb", "rstdb"), writes=("lnbb",), tag=f"lnbb{gti}")
                E("vector", lambda e: e.tensor_scalar(
                    out=xn_own[:], in0=xn_own[:], scalar1=t2c[:], scalar2=t3c[:],
                    op0=ALU.mult, op1=ALU.add),
                    reads=("xn_own_raw",), sreads=("lnbb", "rstdb"), writes=("xn_own",),
                    tag=f"lnab{gti}")
                for br in range(2):
                    gb, gbn = (g1b, "g1b") if br == 0 else (g2b, "g2b")
                    nfm = n1fm if br == 0 else n2fm
                    nfmn = f"n{br+1}fm_t{tt}"
                    E("vector", lambda e, gb=gb: e.tensor_tensor(
                        out=y_f[:], in0=xn_own[:], in1=gb[:], op=ALU.mult),
                        reads=("xn_own",) + tuple(gbn + str(q) for q in range(4)),
                        writes=("y_f",), tag=f"yb{gti}_{br}")
                    E("vector", lambda e: e.reduce_max(out=t1c[:], in_=y_f[:], axis=AX.X,
                                                       apply_absolute_value=True),
                      reads=("y_f",), writes=("mraw",), tag=f"mr{gti}_{br}")
                    E("vector", lambda e: e.tensor_tensor(out=t3c[:], in0=t1c[:], in1=t2c[:],
                                                          op=ALU.mult),
                      reads=("mraw", "rstdb"), writes=("m1",), tag=f"m1{gti}_{br}")
                    E("vector", lambda e: e.tensor_single_scalar(
                        out=t3c[:], in_=t3c[:], scalar=1e-4, op=ALU.max),
                        reads=("m1",), writes=("m1c",), tag=f"m1c{gti}_{br}")
                    E("vector", lambda e: e.reciprocal(out=t4c[:], in_=t3c[:]),
                      reads=("m1c",), writes=("invm",), tag=f"inv{gti}_{br}")
                    E("vector", lambda e: e.scalar_tensor_tensor(
                        out=t4c[:], in0=t4c[:], scalar=127.0, in1=t2c[:],
                        op0=ALU.mult, op1=ALU.mult),
                        reads=("invm", "rstdb"), writes=("c1",), tag=f"c1{gti}_{br}")
                    E("vector", lambda e, br=br, gti=gti: e.scalar_tensor_tensor(
                        out=rs_ar[:, br * 16 + gti: br * 16 + gti + 1], in0=t3c[:],
                        scalar=1.0 / 254.0, in1=scol[:, 6 + br:7 + br],
                        op0=ALU.mult, op1=ALU.mult),
                        reads=("m1c", "scol"), writes=(f"rs{br}_{gti}",), tag=f"rs{gti}_{br}")
                    for hf in range(2):
                        quant_chain(y_f[:, hf * 1024:(hf + 1) * 1024], "y_f",
                                    t4c[:], "c1",
                                    n_bf[:, hf * 1024:(hf + 1) * 1024], f"n_bf{hf}",
                                    1024, (tq_f, ti_q, tf_q, w01_q), f"q{gti}_{br}_{hf}")
                    if debug and gti == 0 and br == 1:
                        DMA("gpsimd", lambda e: e.dma_start(out=dbg["n1"], in_=n_bf[:]),
                            reads=("n_bf0", "n_bf1"), tag="dma_dbgn1")

                    for cb in range(4):
                        pb = cb % 2
                        pbn = f"ptb{pb}"
                        for jj in range(4):
                            c = cb * 4 + jj
                            E("tensor", lambda e, c=c, jj=jj, pb=pb: e.transpose(
                                out=ptb[pb][:, jj * 128:(jj + 1) * 128],
                                in_=n_bf[:, c * 128:(c + 1) * 128], identity=ident[:]),
                                reads=(f"n_bf{c // 8}", "ident"), writes=(pbn,),
                                tag=f"nT{gti}_{br}_{c}")
                        E("vector", lambda e, cb=cb, pb=pb, nfm=nfm, tt=tt: e.tensor_copy(
                            out=nfm.ap().rearrange("p (c n) -> p c n", c=16)[
                                :, cb * 4:(cb + 1) * 4, tt * 128:(tt + 1) * 128],
                            in_=ptb[pb][:].rearrange("p (j n) -> p j n", j=4)),
                            reads=(pbn,), writes=(nfmn + f"_{cb}",), tag=f"nTe{gti}_{br}_{cb}")

            P.set_barrier()
            for sl in range(16):
                for wix, (wt_src, Tt, Ttn) in enumerate(((w1t, T1t, "T1t"), (w2t, T2t, "T2t"))):
                    wsrcv = wt_src.rearrange("(c p) o -> p c o", p=128)
                    for hf in range(4):
                        ternary_tiles(
                            wix, wstg_b, "wstg_b",
                            wstg_b.ap().rearrange("p (c n) -> p c n", c=4),
                            sgA[:, 0:4 * 512], sgB[:, 0:4 * 512],
                            Tt[:, hf * 4 * 512:(hf + 1) * 4 * 512], f"{Ttn}_{hf}",
                            wsrcv[:, hf * 4:(hf + 1) * 4, sl * 512:(sl + 1) * 512],
                            4 * 512, f"tern{tg}_{sl}_{wix}_{hf}")
                DMA("sync", lambda e, sl=sl: e.dma_start(
                    out=g3rsl[:], in_=g3[0:1, sl * 512:(sl + 1) * 512]),
                    writes=("g3rsl",), tag=f"dma_g3r{tg}_{sl}")
                bcast_row(f"g3b{tg}_{sl}", "g3rsl", g3rsl[:], 512, "ps4", ps[4][:])
                E("vector", lambda e: e.tensor_copy(out=g3b[:], in_=ps[4][:]),
                  reads=("ps4",), writes=("g3b",), tag=f"g3be{tg}_{sl}")
                if debug and tg == 0 and sl == 0:
                    DMA("gpsimd", lambda e: e.dma_start(out=dbg["t1t"], in_=T2t[:, 0:512]),
                        reads=("T2t_0",), tag="dma_dbgt1t")
                for tt in range(8):
                    gti = tg * 8 + tt
                    pmA = ps[0 if tt % 2 == 0 else 2]
                    pmB = ps[1 if tt % 2 == 0 else 3]
                    pmAn = "ps0" if tt % 2 == 0 else "ps2"
                    pmBn = "ps1" if tt % 2 == 0 else "ps3"
                    for c in range(16):
                        E("tensor", lambda e, c=c, tt=tt, pmA=pmA: e.matmul(
                            out=pmA[:], lhsT=n1fm[:, c * 1024 + tt * 128: c * 1024 + (tt + 1) * 128],
                            rhs=T1t[:, c * 512:(c + 1) * 512],
                            start=(c == 0), stop=(c == 15)),
                            reads=("T1t_0", "T1t_1", "T1t_2", "T1t_3"), writes=(pmAn,),
                            tag=f"m1_{tg}_{sl}_{tt}_{c}")
                    for c in range(16):
                        E("tensor", lambda e, c=c, tt=tt, pmB=pmB: e.matmul(
                            out=pmB[:], lhsT=n2fm[:, c * 1024 + tt * 128: c * 1024 + (tt + 1) * 128],
                            rhs=T2t[:, c * 512:(c + 1) * 512],
                            start=(c == 0), stop=(c == 15)),
                            reads=("T2t_0", "T2t_1", "T2t_2", "T2t_3"), writes=(pmBn,),
                            tag=f"m2_{tg}_{sl}_{tt}_{c}")
                    E("scalar", lambda e, pmA=pmA, gti=gti: e.activation(
                        out=sw_f[:], in_=pmA[:], func=AF.Silu,
                        scale=rs_ar[:, gti:gti + 1]),
                        reads=(pmAn,), sreads=(f"rs0_{gti}",), writes=("sw_f",),
                        tag=f"sw{tg}_{sl}_{tt}")
                    E("scalar", lambda e, pmB=pmB, gti=gti: e.activation(
                        out=xv_f[:], in_=pmB[:], func=AF.Copy,
                        scale=rs_ar[:, 16 + gti:16 + gti + 1]),
                        reads=(pmBn,), sreads=(f"rs1_{gti}",), writes=("xv_f",),
                        tag=f"xv{tg}_{sl}_{tt}")
                    if debug and tg == 0 and sl == 0 and tt == 0:
                        DMA("gpsimd", lambda e: e.dma_start(out=dbg["swx"][:, 0:512], in_=sw_f[:]),
                            reads=("sw_f",), tag="dma_dbgsw")
                        DMA("gpsimd", lambda e: e.dma_start(out=dbg["swx"][:, 512:1024], in_=xv_f[:]),
                            reads=("xv_f",), tag="dma_dbgxv")
                    E("vector", lambda e: e.tensor_tensor(out=h_f[:], in0=sw_f[:],
                                                          in1=xv_f[:], op=ALU.mult),
                      reads=("sw_f", "xv_f"), writes=("h_f",), tag=f"h{tg}_{sl}_{tt}")
                    yb = y3_f[:, (tt % 2) * 512:(tt % 2 + 1) * 512]
                    ybn = f"y3_{tt % 2}"
                    E("gpsimd", lambda e, yb=yb: e.tensor_tensor(
                        out=yb, in0=h_f[:], in1=g3b[:], op=ALU.mult),
                        reads=("h_f", "g3b"), writes=(ybn,), k=1, tag=f"y3{tg}_{sl}_{tt}")
                    E("scalar", lambda e: e.activation(out=sqd_f[:], in_=h_f[:],
                                                       func=AF.Square, accum_out=t1c[:]),
                      reads=("h_f",), writes=("sqd", "sqtmp"), tag=f"sq{tg}_{sl}_{tt}")
                    E("vector", lambda e, gti=gti: e.tensor_tensor(
                        out=ssq3_ar[:, gti:gti + 1], in0=ssq3_ar[:, gti:gti + 1],
                        in1=t1c[:], op=ALU.add),
                        reads=("sqtmp", "ssq3"), writes=("ssq3",), tag=f"sqa{tg}_{sl}_{tt}")
                    E("vector", lambda e, yb=yb: e.reduce_max(
                        out=t3c[:], in_=yb, axis=AX.X, apply_absolute_value=True),
                        reads=(ybn,), writes=("mxtmp",), tag=f"mx{tg}_{sl}_{tt}")
                    E("vector", lambda e, gti=gti: e.tensor_tensor(
                        out=mxy3_ar[:, gti:gti + 1], in0=mxy3_ar[:, gti:gti + 1],
                        in1=t3c[:], op=ALU.max),
                        reads=("mxtmp", "mxy3"), writes=("mxy3",), tag=f"mxa{tg}_{sl}_{tt}")
                    DMA("gpsimd", lambda e, gti=gti, sl=sl, yb=yb: e.dma_start(
                        out=h_scr[gti * 128:(gti + 1) * 128, sl * 512:(sl + 1) * 512],
                        in_=yb),
                        reads=(ybn,), writes=(f"hscr{gti}_{sl}",), tag=f"dma_h{tg}_{sl}_{tt}")

            if debug and tg == 0:
                DMA("gpsimd", lambda e: e.dma_start(out=dbg["hrow"], in_=h_scr[0:128, :]),
                    reads=tuple(f"hscr0_{sl}" for sl in range(16)), tag="dma_dbghrow")
            P.set_barrier()
            c0, c1_ = tg * 8, tg * 8 + 8
            E("vector", lambda e, c0=c0, c1_=c1_: e.tensor_single_scalar(
                out=rstd3_ar[:, c0:c1_], in_=ssq3_ar[:, c0:c1_], scalar=1.0 / HID,
                op=ALU.mult),
                reads=("ssq3",), writes=("ms3", "sd3", "rstd3"), tag=f"ms3_{tg}")
            E("scalar", lambda e, c0=c0, c1_=c1_: e.activation(out=rstd3_ar[:, c0:c1_],
                                               in_=rstd3_ar[:, c0:c1_],
                                               func=AF.Sqrt, bias=cst[:, 2:3]),
              reads=("ms3",), sreads=("cst",), writes=("sd3",), tag=f"sd3_{tg}")
            E("vector", lambda e, c0=c0, c1_=c1_: e.reciprocal(out=rstd3_ar[:, c0:c1_],
                                               in_=rstd3_ar[:, c0:c1_]),
              reads=("sd3",), writes=("rstd3",), tag=f"rstd3_{tg}")
            E("vector", lambda e, c0=c0, c1_=c1_: e.tensor_tensor(
                out=c3_ar[:, c0:c1_], in0=mxy3_ar[:, c0:c1_], in1=rstd3_ar[:, c0:c1_],
                op=ALU.mult),
                reads=("mxy3", "rstd3"), writes=("m3", "m3c", "inv3", "c3"), tag=f"m3_{tg}")
            E("vector", lambda e, c0=c0, c1_=c1_: e.tensor_single_scalar(
                out=c3_ar[:, c0:c1_], in_=c3_ar[:, c0:c1_], scalar=1e-4, op=ALU.max),
                reads=("m3",), writes=("m3c",), tag=f"m3c_{tg}")
            for tt in range(8):
                gti = tg * 8 + tt
                E("vector", lambda e, gti=gti: e.scalar_tensor_tensor(
                    out=rs_ar[:, 32 + gti:33 + gti], in0=c3_ar[:, gti:gti + 1],
                    scalar=1.0 / 254.0, in1=scol[:, 8:9], op0=ALU.mult, op1=ALU.mult),
                    reads=("m3c", "scol"), writes=(f"rs2_{gti}",), tag=f"rs3_{gti}")
            E("vector", lambda e, c0=c0, c1_=c1_: e.reciprocal(out=c3_ar[:, c0:c1_], in_=c3_ar[:, c0:c1_]),
              reads=("m3c",), writes=("inv3",), tag=f"inv3_{tg}")
            E("vector", lambda e, c0=c0, c1_=c1_: e.scalar_tensor_tensor(
                out=c3_ar[:, c0:c1_], in0=c3_ar[:, c0:c1_], scalar=127.0,
                in1=rstd3_ar[:, c0:c1_], op0=ALU.mult, op1=ALU.mult),
                reads=("inv3", "rstd3"), writes=("c3",), tag=f"c3_{tg}")

            for sl in range(8):
                w3v = w3t.rearrange("(c p) n -> p c n", p=128)
                for qq in range(8):
                    ternary_tiles(
                        2, w3stg, "w3stg",
                        w3stg.ap()[:, :],
                        sg3A[:, 0:D], sg3B[:, 0:D],
                        T3t[:, qq * D:(qq + 1) * D], f"T3t_{qq % 4}",
                        w3v[:, sl * 8 + qq, :],
                        D, f"t3n{tg}_{sl}_{qq}")
                for tt in range(8):
                    gti = tg * 8 + tt
                    yb = y3t[:, (tt % 2) * 1024:(tt % 2 + 1) * 1024]
                    ybn = f"y3t_{tt % 2}"
                    DMA("sync", lambda e, gti=gti, sl=sl, yb=yb: e.dma_start(
                        out=yb, in_=h_scr[gti * 128:(gti + 1) * 128,
                                          sl * 1024:(sl + 1) * 1024]),
                        reads=(f"hscr{gti}_{2 * sl}", f"hscr{gti}_{2 * sl + 1}"),
                        writes=(ybn,), tag=f"dma_hr{tg}_{sl}_{tt}")
                    quant_chain(yb, ybn, c3_ar[:, gti:gti + 1], "c3",
                                n3_bf[:], "n3_bf", 1024,
                                (t3q_f, ti3, tf3, w013), f"q3{tg}_{sl}_{tt}")
                    if debug and tg == 0 and sl == 0 and tt == 0:
                        DMA("gpsimd", lambda e: e.dma_start(
                            out=dbg["probe"][:, 0:1024], in_=n3_bf[:]),
                            reads=("n3_bf",), tag="dma_dbgn3")
                        DMA("gpsimd", lambda e: e.dma_start(
                            out=dbg["probe"][:, 1024:1040], in_=c3_ar[:]),
                            reads=("c3",), tag="dma_dbgc3")
                        DMA("gpsimd", lambda e: e.dma_start(
                            out=dbg["probe"][:, 1040:1056], in_=rstd3_ar[:]),
                            reads=("rstd3",), tag="dma_dbgrstd3")
                        DMA("gpsimd", lambda e: e.dma_start(
                            out=dbg["probe"][:, 1056:1072], in_=ssq3_ar[:]),
                            reads=("ssq3",), tag="dma_dbgssq3")
                        DMA("gpsimd", lambda e: e.dma_start(
                            out=dbg["probe"][:, 1072:2096], in_=yb),
                            reads=(ybn,), tag="dma_dbgyb")
                    for cb in range(2):
                        pb = cb % 2
                        pbn = f"ptb{pb}"
                        for jj in range(4):
                            c = cb * 4 + jj
                            E("tensor", lambda e, c=c, jj=jj, pb=pb: e.transpose(
                                out=ptb[pb][:, jj * 128:(jj + 1) * 128],
                                in_=n3_bf[:, c * 128:(c + 1) * 128], identity=ident[:]),
                                reads=("n3_bf", "ident"), writes=(pbn,),
                                tag=f"n3T{tg}_{sl}_{tt}_{c}")
                        E("vector", lambda e, cb=cb, pb=pb: e.tensor_copy(
                            out=n3fm[:, cb * 512:(cb + 1) * 512], in_=ptb[pb][:]),
                            reads=(pbn,), writes=(f"n3fm_{cb}",), tag=f"n3Te{tg}_{sl}_{tt}_{cb}")
                    for q in range(4):
                        for c in range(8):
                            E("tensor", lambda e, q=q, c=c: e.matmul(
                                out=ps[q][:], lhsT=n3fm[:, c * 128:(c + 1) * 128],
                                rhs=T3t[:, c * D + q * 512: c * D + (q + 1) * 512],
                                start=(c == 0), stop=(c == 7)),
                                reads=(f"n3fm_{c // 4}",) + tuple(f"T3t_{qq}" for qq in range(4)),
                                writes=(f"ps{q}",), tag=f"m3_{tg}_{sl}_{tt}_{q}_{c}")
                    for q in range(4):
                        if sl == 0:
                            E("vector", lambda e, q=q, tt=tt: e.tensor_copy(
                                out=o3acc[:, tt * D + q * 512: tt * D + (q + 1) * 512],
                                in_=ps[q][:]),
                                reads=(f"ps{q}",), writes=(f"o3_{tt}_{q}",),
                                tag=f"o3c{tg}_{sl}_{tt}_{q}")
                        else:
                            E("vector", lambda e, q=q, tt=tt: e.tensor_tensor(
                                out=o3acc[:, tt * D + q * 512: tt * D + (q + 1) * 512],
                                in0=o3acc[:, tt * D + q * 512: tt * D + (q + 1) * 512],
                                in1=ps[q][:], op=ALU.add),
                                reads=(f"ps{q}", f"o3_{tt}_{q}"), writes=(f"o3_{tt}_{q}",),
                                tag=f"o3a{tg}_{sl}_{tt}_{q}")

            if debug and tg == 0:
                DMA("gpsimd", lambda e: e.dma_start(out=dbg["o3"], in_=o3acc[:, 0:D]),
                    reads=tuple(f"o3_0_{q}" for q in range(4)), tag="dma_dbgo3")
            P.set_barrier()
            for tt in range(8):
                gti = tg * 8 + tt
                db = tt % 2
                xo = xo_t[:, db * D:(db + 1) * D]
                ao = ao_t[:, db * D:(db + 1) * D]
                ost = os_t[:, db * D:(db + 1) * D]
                DMA("sync", lambda e, gti=gti, xo=xo: e.dma_start(
                    out=xo, in_=xo_in[gti * 128:(gti + 1) * 128, :]),
                    writes=(f"fxo{db}",), tag=f"dma_fx{gti}")
                DMA("gpsimd", lambda e, gti=gti, ao=ao: e.dma_start(
                    out=ao, in_=attown_scr[gti * 128:(gti + 1) * 128, :]),
                    reads=(f"aow{gti}",) if gti < nseg else (),
                    writes=(f"fao{db}",), tag=f"dma_fa{gti}")
                E("vector", lambda e, tt=tt, xo=xo, ost=ost, gti=gti: e.scalar_tensor_tensor(
                    out=ost, in0=o3acc[:, tt * D:(tt + 1) * D],
                    scalar=rs_ar[:, 32 + gti:33 + gti], in1=xo,
                    op0=ALU.mult, op1=ALU.add),
                    reads=(f"fxo{db}",) + tuple(f"o3_{tt}_{q}" for q in range(4)),
                    sreads=(f"rs2_{gti}",),
                    writes=(f"fs1_{db}",), tag=f"f1_{gti}")
                E("vector", lambda e, ao=ao, ost=ost: e.tensor_tensor(
                    out=ost, in0=ost, in1=ao, op=ALU.add),
                    reads=(f"fao{db}", f"fs1_{db}"), writes=(f"fs2_{db}",), tag=f"f2_{gti}")
                DMA("gpsimd", lambda e, gti=gti, ost=ost: e.dma_start(
                    out=out[gti * 128:(gti + 1) * 128, :], in_=ost),
                    reads=(f"fs2_{db}",), writes=(f"outw{gti}",), tag=f"dma_out{gti}")

    import os as _os
    P.truncate(int(_os.environ.get("MAXOPS", "0")) or None)
    P.finalize()
    blk = ctx.enter_context(nc.Block())
    P.build_streams(blk, sems)
    ctx.close()
    return nc, dbg


def build_phase_b(nc, P, E, DMA, env):
    raise NotImplementedError


import ml_dtypes
from concourse.bass_utils import run_bass_kernel_spmd

_IDENT = np.eye(128, dtype=ml_dtypes.bfloat16)
_DMASK = np.where(np.tril(np.ones((128, 128), bool)), 0.0, -1.0e30).astype(np.float32)
_MEMINIT = np.zeros((128, 130), np.float32)
_MEMINIT[:, 128] = 1.0 / 128.0
_CST = np.tile(np.array([[EPS_LN, 1.0, EPS_FFN]], np.float32), (128, 1))


def kernel(L0_attn_x, L0_attn_Wq, L0_attn_Wk, L0_attn_Wv, L0_attn_Wo, L0_attn_betas,
           L0_ffn_w1, L0_ffn_g1, L0_ffn_w2, L0_ffn_g2, L0_ffn_w3, L0_ffn_g3):
    f = np.float32
    ax = lambda a: np.ascontiguousarray(np.asarray(a, dtype=f))
    xfull = ax(L0_attn_x)
    Wq, Wk, Wv, Wo = ax(L0_attn_Wq), ax(L0_attn_Wk), ax(L0_attn_Wv), ax(L0_attn_Wo)
    betas = ax(L0_attn_betas)
    w1t = ax(np.asarray(L0_ffn_w1).T)
    w2t = ax(np.asarray(L0_ffn_w2).T)
    w3t = ax(np.asarray(L0_ffn_w3).T)
    g1 = ax(L0_ffn_g1)[None, :]
    g2 = ax(L0_ffn_g2)[None, :]
    g3 = ax(L0_ffn_g3)[None, :]

    nc, _dbg = build_core_program(0, debug=0)

    in_maps = []
    for c in range(8):
        b, cp = c // 4, c % 4
        h0 = 4 * cp
        xb = np.ascontiguousarray(xfull[b])
        xo = np.ascontiguousarray(
            xb.reshape(16, 4, 128, D)[:, cp].reshape(16 * 128, D))
        in_maps.append({
            "x": xb, "xo": xo,
            "wq": np.ascontiguousarray(Wq[:, h0 * 128:(h0 + 4) * 128]),
            "wk": np.ascontiguousarray(Wk[:, h0 * 128:(h0 + 4) * 128]),
            "wv": np.ascontiguousarray(Wv[:, h0 * 128:(h0 + 4) * 128]),
            "wo": Wo,
            "betasT": np.ascontiguousarray(betas[0, h0:h0 + 4, 0, :].T),
            "w1t": w1t, "w2t": w2t, "w3t": w3t,
            "g1": g1, "g2": g2, "g3": g3,
            "ident": _IDENT, "dmask": _DMASK, "meminit": _MEMINIT, "cst": _CST,
        })
    res = run_bass_kernel_spmd(nc, in_maps, core_ids=list(range(8)))
    out = np.empty((2, S, D), f)
    for c in range(8):
        b, cp = c // 4, c % 4
        rr = np.asarray(res.results[c]["out"], dtype=f)
        out[b].reshape(16, 4, 128, D)[:, cp] = rr.reshape(16, 128, D)
    return out


# version 04
# speedup vs baseline: 1.0273x; 1.0273x over previous
import numpy as np
from contextlib import ExitStack
import concourse.bass as bass
import concourse.mybir as mybir

F32 = mybir.dt.float32
BF16 = mybir.dt.bfloat16
I32 = mybir.dt.int32
AF = mybir.ActivationFunctionType
ALU = mybir.AluOpType
AX = mybir.AxisListType

D = 2048
S = 8192
SEG = 512
NSEG = 16
HPC = 4
DK = DV = 128
HID = 8192
EPS_LN = 1e-5
EPS_FFN = 1e-8
ATANH_HALF = 0.5493061443340549
INV_SC = float(1.0 / np.sqrt(128.0))
NW_ELEM = float(HID * D)

ENGINES = ("sync", "vector", "scalar", "tensor", "gpsimd")
ESEM = {"sync": "dx", "gpsimd": "po", "vector": "dv", "scalar": "ac", "tensor": "pe"}
DSEM = {"sync": "dx", "gpsimd": "dg", "scalar": "ds"}


def mmax(*marks):
    out = {}
    for m in marks:
        if not m:
            continue
        for s, v in m.items():
            out[s] = max(out.get(s, 0), v)
    return out


class Prog:
    def __init__(self):
        self.ops = []
        self.tot = {}
        self.buf = {}

    def set_barrier(self):
        self.barrier_pending = {e: dict(self.tot) for e in ENGINES}

    def emit(self, engine, fn, reads=(), sreads=(), writes=(), inc=(), tag=""):
        w_data = {}
        w_strict = {}
        bp = getattr(self, "barrier_pending", None)
        if bp and engine in bp:
            w_data = mmax(w_data, bp.pop(engine))
        for b in reads:
            st = self.buf.get(b)
            if st:
                w_data = mmax(w_data, st["prod"])
        for b in sreads:
            st = self.buf.get(b)
            if st:
                w_strict = mmax(w_strict, st["prod"])
        for b in writes:
            st = self.buf.get(b)
            if st:
                w_data = mmax(w_data, st["cons"], st["prod"])
        inc = tuple(inc)
        assert inc, f"op without inc: {tag}"
        for s, k in inc:
            self.tot[s] = self.tot.get(s, 0) + k
        mark = {s: self.tot[s] for s, _ in inc}
        self.ops.append((engine, fn, dict(w_data), dict(w_strict), inc, tag))
        for b in writes:
            self.buf[b] = {"prod": dict(mark), "cons": {}}
        for b in tuple(reads) + tuple(sreads):
            st = self.buf.get(b)
            if st is None:
                self.buf[b] = {"prod": {}, "cons": dict(mark)}
            else:
                st["cons"] = mmax(st["cons"], mark)
        return mark

    def now(self, sem):
        return self.tot.get(sem, 0)

    def truncate(self, n):
        if n is None or n >= len(self.ops):
            return
        self.ops = self.ops[:n]
        tot = {}
        for _, _, _, _, inc, _ in self.ops:
            for s, k in inc:
                tot[s] = tot.get(s, 0) + k
        self.tot = tot

    def finalize(self):
        used = {o[0] for o in self.ops}
        allw = {s: v for s, v in self.tot.items()}
        for eng_name in ENGINES:
            if eng_name in used:
                self.ops.append((eng_name, None, {}, dict(allw), (), "final_wait"))

    def build_streams(self, blk, sems):
        for eng_name in ENGINES:
            ops = [o for o in self.ops if o[0] == eng_name]
            if not ops:
                continue
            own = ESEM[eng_name]

            def mk(ops_, own_):
                def body(eng):
                    drop_own = own_ == "pe"
                    for _, fn, w_data, w_strict, inc, tag in ops_:
                        w = dict(w_strict)
                        for s, v in w_data.items():
                            if s == own_ and drop_own:
                                continue
                            w[s] = max(w.get(s, 0), v)
                        if fn is None:
                            w.pop(own_, None)
                        for s, v in sorted(w.items()):
                            if v > 0:
                                eng.wait_ge(sems[s], v)
                        if fn is None:
                            continue
                        inst = fn(eng)
                        for s, k in inc:
                            assert inst is not None, f"then_inc on None: {tag}"
                            inst = inst.then_inc(sems[s], k)
                return body
            getattr(blk, eng_name)(mk(ops, own))


def build_core_program(core_id, debug=0, nseg=NSEG, do_ap=True, do_b=True):
    nc = bass.Bass("TRN2", target_bir_lowering=False, detect_race_conditions=False)
    P = Prog()
    dbg = {}
    p_ar_ref = [None]
    cp = core_id % 4
    own0 = 128 * cp

    def E(engine, fn, reads=(), sreads=(), writes=(), k=None, tag=""):
        sem = ESEM[engine]
        if k is None:
            k = 16 if engine in ("sync", "gpsimd") and "dma" in tag else 1
        return P.emit(engine, fn, reads=reads, sreads=sreads, writes=writes,
                      inc=[(sem, k)], tag=tag)

    def DMA(engine, fn, reads=(), writes=(), tag=""):
        return P.emit(engine, fn, reads=reads, writes=writes,
                      inc=[(DSEM[engine], 16)], tag=tag)

    x = nc.dram_tensor("x", [S, D], F32, kind="ExternalInput").ap()
    xo_in = nc.dram_tensor("xo", [NSEG * 128, D], F32, kind="ExternalInput").ap()
    wq = nc.dram_tensor("wq", [D, HPC * DK], F32, kind="ExternalInput").ap()
    wk = nc.dram_tensor("wk", [D, HPC * DK], F32, kind="ExternalInput").ap()
    wv = nc.dram_tensor("wv", [D, HPC * DK], F32, kind="ExternalInput").ap()
    wo = nc.dram_tensor("wo", [D, D], F32, kind="ExternalInput").ap()
    betasT = nc.dram_tensor("betasT", [DV, HPC], F32, kind="ExternalInput").ap()
    w1t = nc.dram_tensor("w1t", [D, HID], F32, kind="ExternalInput").ap()
    w2t = nc.dram_tensor("w2t", [D, HID], F32, kind="ExternalInput").ap()
    w3t = nc.dram_tensor("w3t", [HID, D], F32, kind="ExternalInput").ap()
    g1 = nc.dram_tensor("g1", [1, D], F32, kind="ExternalInput").ap()
    g2 = nc.dram_tensor("g2", [1, D], F32, kind="ExternalInput").ap()
    g3 = nc.dram_tensor("g3", [1, HID], F32, kind="ExternalInput").ap()
    ident_in = nc.dram_tensor("ident", [128, 128], BF16, kind="ExternalInput").ap()
    dmask_in = nc.dram_tensor("dmask", [128, 128], F32, kind="ExternalInput").ap()
    meminit_in = nc.dram_tensor("meminit", [128, 130], F32, kind="ExternalInput").ap()
    cst_in = nc.dram_tensor("cst", [128, 3], F32, kind="ExternalInput").ap()

    out = nc.dram_tensor("out", [NSEG * 128, D], F32, kind="ExternalOutput").ap()

    attfm_scr = nc.dram_tensor("attfm_scr", [NSEG * HPC, 128, SEG], BF16, kind="Internal").ap()
    attown_scr = nc.dram_tensor("attown_scr", [NSEG * 128, D], F32, kind="Internal").ap()
    h_scr = nc.dram_tensor("h_scr", [NSEG * 128, HID], F32, kind="Internal").ap()
    sred_scr = nc.dram_tensor("sred_scr", [1, 384], F32, kind="Internal").ap()

    if debug:
        for nm, shp, dt in (("xnfm", [16, 128, SEG], BF16), ("qfm", [128, SEG], BF16),
                            ("sq", [128, SEG], BF16), ("vtm", [128, 4 * SEG], BF16),
                            ("p0", [128, 4 * SEG], BF16),
                            ("attfm", [NSEG * HPC, 128, SEG], BF16),
                            ("attown", [NSEG * 128, D], F32),
                            ("svals", [128, 12], F32),
                            ("n1", [128, D], BF16),
                            ("t1t", [128, 512], BF16),
                            ("hrow", [128, HID], F32),
                            ("o3", [128, D], F32),
                            ("swx", [128, 1024], F32),
                            ("probe", [128, 4096], F32)):
            dbg[nm] = nc.dram_tensor("dbg_" + nm, shp, dt, kind="ExternalOutput").ap()

    ctx = ExitStack()

    def sb(name, shape, dt):
        return nc.alloc_sbuf_tensor(name, shape, dt)

    ident = sb("c_ident", [128, 128], BF16)
    cst = sb("c_cst", [128, 3], F32)
    dmask = sb("c_dmask", [128, 128], F32)
    onesrow = sb("c_onesrow", [1, 128], BF16)
    beta_c = sb("c_beta", [128, HPC], F32)
    ombeta_c = sb("c_ombeta", [128, HPC], F32)
    scol = sb("c_scol", [128, 12], F32)
    spart = sb("c_spart", [128, 4], F32)
    stmp = sb("c_stmp", [128, 1], F32)
    srow384 = sb("c_srow384", [1, 384], F32)
    svrow = sb("c_svrow", [1, 16], F32)
    hl_hi = sb("c_hlhi", [1, 512], BF16)
    hl_lo = sb("c_hllo", [1, 512], BF16)
    hl_tmp = sb("c_hltmp", [1, 512], F32)

    ps = [nc.alloc_psum_tensor(f"ps{i}", [128, 512], F32) for i in range(6)]
    ptbA = nc.alloc_psum_tensor("ptbA", [128, 512], BF16)
    ptbB = nc.alloc_psum_tensor("ptbB", [128, 512], BF16)
    ptb = [ptbA[:], ptbB[:]]

    sems = {}
    for s_name in ("dx", "dg", "ds", "pe", "dv", "ac", "po", "fin"):
        sems[s_name] = ctx.enter_context(nc.semaphore("sem_" + s_name))

    def bcast_row(tag, row_buf, row_ap, n, pdst_name, pdst_ap):
        E("vector", lambda e: e.tensor_copy(out=hl_hi[:, 0:n], in_=row_ap),
          reads=(row_buf,), writes=("hl_hi",), tag=tag + "_hi")
        E("vector", lambda e: e.tensor_copy(out=hl_tmp[:, 0:n], in_=hl_hi[:, 0:n]),
          reads=("hl_hi",), writes=("hl_tmp",), tag=tag + "_hf")
        E("vector", lambda e: e.tensor_tensor(out=hl_lo[:, 0:n], in0=row_ap,
                                              in1=hl_tmp[:, 0:n], op=ALU.subtract),
          reads=(row_buf, "hl_tmp"), writes=("hl_lo",), tag=tag + "_lo")
        E("tensor", lambda e: e.matmul(out=pdst_ap, lhsT=onesrow[:], rhs=hl_hi[:, 0:n],
                                       start=True, stop=False),
          reads=("hl_hi", "onesrow"), writes=(pdst_name,), tag=tag + "_m1")
        return E("tensor", lambda e: e.matmul(out=pdst_ap, lhsT=onesrow[:], rhs=hl_lo[:, 0:n],
                                              start=False, stop=True),
                 reads=("hl_lo",), writes=(pdst_name,), tag=tag + "_m2")

    DMA("sync", lambda e: e.dma_start(out=ident[:], in_=ident_in), writes=("ident",), tag="dma_ident")
    DMA("sync", lambda e: e.dma_start(out=cst[:], in_=cst_in), writes=("cst",), tag="dma_cst")
    DMA("sync", lambda e: e.dma_start(out=dmask[:], in_=dmask_in), writes=("dmask",), tag="dma_dmask")
    DMA("sync", lambda e: e.dma_start(out=beta_c[:], in_=betasT), writes=("beta_raw",), tag="dma_beta")
    E("scalar", lambda e: e.activation(out=beta_c[:], in_=beta_c[:], func=AF.Sigmoid),
      reads=("beta_raw",), writes=("beta",), tag="sig_beta")
    E("scalar", lambda e: e.activation(out=ombeta_c[:], in_=beta_c[:], func=AF.Copy,
                                       bias=1.0, scale=-1.0),
      reads=("beta",), writes=("ombeta",), tag="ombeta")
    E("vector", lambda e: e.memset(onesrow[:], 1.0), writes=("onesrow",), tag="ones_r")
    E("vector", lambda e: e.memset(spart[:], 0.0), writes=("spart",), tag="spart0")
    if debug:
        P.emit("vector", lambda e: e.memset(p_ar_ref[0][:], 0.0), writes=("p0", "p1", "p2", "p3"),
               inc=[("dv", 1)], tag="p_ar0")

    sbuf_base_A = nc.sbuf_base

    def sba(name, shape, dt):
        return nc.alloc_sbuf_tensor(name, shape, dt)

    wq_sb = sba("a_wq", [128, 16 * 512], BF16)
    wk_sb = sba("a_wk", [128, 16 * 512], BF16)
    wv_sb = sba("a_wv", [128, 16 * 512], BF16)
    x4 = sba("a_x4", [128, 4 * D], F32)
    xn_tm = sba("a_xn_tm", [128, 4 * D], BF16)
    xn_fm = sba("a_xn_fm", [128, 16 * 512], BF16)
    v_bf = sba("a_v", [128, 4 * 512], BF16)
    q_bf = sba("a_q", [128, 512], BF16)
    k_bf = sba("a_k", [128, 512], BF16)
    sq_bf = sba("a_sq", [128, 512], BF16)
    sk_bf = sba("a_sk", [128, 512], BF16)
    sk_tm = sba("a_sktm", [128, 512], BF16)
    eq_f = sba("a_eq", [128, 512], F32)
    ek_f = sba("a_ek", [128, 512], F32)
    p_ar = sba("a_p", [128, 4 * 512], BF16)
    p_ar_ref[0] = p_ar
    pTa = sba("a_pTa", [128, 512], BF16)
    accs = sba("a_accs", [128, 4], F32)
    rl_bf = sba("a_rlbf", [128, 4], BF16)
    rl_f = sba("a_rlf", [128, 4], F32)
    rlrow_bf = sba("a_rlrow", [1, 512], BF16)
    rnrow_f = sba("a_rnrow", [1, 512], F32)
    mem_f = sba("a_memf", [128, 132 * HPC], F32)
    memz_bf = sba("a_memz", [128, 132], BF16)
    delta_bf = sba("a_delta", [128, 512], BF16)
    zups = sba("a_zups", [128, 1], F32)
    u_f = sba("a_u", [128, 512], F32)
    t2_f = sba("a_t2", [128, 512], F32)
    t3_f = sba("a_t3", [128, 512], F32)
    att_sb = sba("a_attbf", [128, 512], BF16)
    ln_st = sba("a_lnst", [128, 24], F32)
    ln_ag = sba("a_lnag", [128, 2], F32)
    sd_c = sba("a_sd", [128, 1], F32)
    rstd_c = sba("a_rstd", [128, 1], F32)
    lnb_c = sba("a_lnb", [128, 1], F32)
    rmax_c = sba("a_rmax", [128, 1], F32)
    ebias_c = sba("a_ebias", [128, 1], F32)
    rkn_c = sba("a_rkn", [128, 1], F32)
    wstg = [sba(f"a_wstg{i}", [128, 2048], F32) for i in range(2)]
    wabs = sba("a_wabs", [128, 2048], F32)
    stmp_ar = sba("a_stmpar", [128, 192], F32)

    def ldw(dst, dstname, src, tag):
        return DMA("gpsimd", lambda e: e.dma_start(
            out=dst.ap().rearrange("p (c n) -> p c n", c=16),
            in_=src.rearrange("(c p) n -> p c n", p=128)), writes=(dstname,), tag=tag)

    ldw(wq_sb, "wq_sb", wq, "dma_wq")
    ldw(wk_sb, "wk_sb", wk, "dma_wk")
    ldw(wv_sb, "wv_sb", wv, "dma_wv")
    for h in range(HPC):
        DMA("sync", lambda e, h=h: e.dma_start(out=mem_f[:, h * 132:h * 132 + 130],
                                               in_=meminit_in),
            writes=(f"mem{h}", f"mem{h}z"), tag=f"dma_mem{h}")

    s_tiles = []
    for wi, wsrc in enumerate((w1t, w2t, w3t)):
        rows, cols = wsrc.shape
        for r0 in range(0, rows, 128):
            for c0 in range(0, cols, 2048):
                s_tiles.append((wi, wsrc[r0:r0 + 128, c0:c0 + 2048]))
    s_i = [0]

    def emit_s_tiles(kk):
        for _ in range(kk):
            i = s_i[0]
            if i >= len(s_tiles):
                return
            s_i[0] += 1
            wi, src = s_tiles[i]
            bufn = f"wstg{i % 2}"
            buf = wstg[i % 2]
            DMA("scalar", lambda e, src=src, buf=buf: e.dma_start(out=buf[:], in_=src),
                writes=(bufn,), tag=f"dma_s{i}")
            E("scalar", lambda e, buf=buf, i=i: e.activation(
                out=wabs[:], in_=buf[:], func=AF.Abs, accum_out=stmp_ar[:, i:i + 1]),
                reads=(bufn,), writes=("wabs", f"stmp{i}"), tag=f"sab{i}")

    NS_PER_SEG = (len(s_tiles) + nseg - 1) // nseg

    for sg in range(nseg):
        xv = x.rearrange("(s j p) d -> s p j d", s=16, j=4, p=128)[sg]
        DMA("sync", lambda e, xv=xv: e.dma_start(
            out=x4.ap().rearrange("p (j d) -> p j d", j=4), in_=xv),
            writes=("x4",), tag=f"dma_x{sg}")

        for j in range(4):
            for kq in range(4):
                E("vector", lambda e, j=j, kq=kq: e.bn_stats(
                    out=ln_st[:, kq * 6:(kq + 1) * 6],
                    in_=x4[:, j * D + kq * 512: j * D + (kq + 1) * 512]),
                    reads=("x4",), writes=("ln_st",), tag=f"bn{sg}_{j}_{kq}")
            E("vector", lambda e: e.bn_aggr(out=ln_ag[:], in_=ln_st[:]),
              reads=("ln_st",), writes=("ln_ag",), tag=f"ag{sg}_{j}")
            E("scalar", lambda e: e.activation(out=sd_c[:], in_=ln_ag[:, 1:2],
                                               func=AF.Sqrt, bias=cst[:, 0:1]),
              reads=("ln_ag",), sreads=("cst",), writes=("sd",), tag=f"lsd{sg}_{j}")
            E("vector", lambda e: e.reciprocal(out=rstd_c[:], in_=sd_c[:]),
              reads=("sd",), writes=("rstd",), tag=f"rstd{sg}_{j}")
            E("vector", lambda e: e.scalar_tensor_tensor(
                out=lnb_c[:], in0=ln_ag[:, 0:1], scalar=-1.0, in1=rstd_c[:],
                op0=ALU.mult, op1=ALU.mult),
                reads=("ln_ag", "rstd"), writes=("lnb",), tag=f"lnb{sg}_{j}")
            E("vector", lambda e, j=j: e.tensor_scalar(
                out=xn_tm[:, j * D:(j + 1) * D], in0=x4[:, j * D:(j + 1) * D],
                scalar1=rstd_c[:], scalar2=lnb_c[:], op0=ALU.mult, op1=ALU.add),
                reads=("x4",), sreads=("lnb", "rstd"), writes=(f"xn_tm{j}",),
                tag=f"lna{sg}_{j}")

        for c in range(16):
            pb = c % 2
            pbn = f"ptb{pb}"
            for j in range(4):
                E("tensor", lambda e, c=c, j=j, pb=pb: e.transpose(
                    out=ptb[pb][:, j * 128:(j + 1) * 128],
                    in_=xn_tm[:, j * D + c * 128: j * D + (c + 1) * 128],
                    identity=ident[:]),
                    reads=(f"xn_tm{j}", "ident"), writes=(pbn,), tag=f"xnT{sg}_{c}_{j}")
            E("vector", lambda e, c=c, pb=pb: e.tensor_copy(
                out=xn_fm[:, c * 512:(c + 1) * 512], in_=ptb[pb][:]),
                reads=(pbn,), writes=(f"xn_fm{c}",), tag=f"xnTe{sg}_{c}")

        if debug and sg == 0:
            DMA("gpsimd", lambda e: e.dma_start(
                out=dbg["xnfm"].rearrange("c p n -> p c n"),
                in_=xn_fm.ap().rearrange("p (c n) -> p c n", c=16)),
                reads=tuple(f"xn_fm{c}" for c in range(16)), tag="dma_dbgxnfm")

        for tt in range(4):
            for c in range(16):
                E("tensor", lambda e, tt=tt, c=c: e.matmul(
                    out=ps[2][:], lhsT=xn_fm[:, c * 512 + tt * 128: c * 512 + (tt + 1) * 128],
                    rhs=wv_sb[:, c * 512:(c + 1) * 512],
                    start=(c == 0), stop=(c == 15)),
                    reads=(f"xn_fm{c}", "wv_sb"), writes=("ps2",), tag=f"vmm{sg}_{tt}_{c}")
            E("scalar", lambda e, tt=tt: e.activation(
                out=v_bf[:, tt * 512:(tt + 1) * 512], in_=ps[2][:], func=AF.Copy),
                reads=("ps2",), writes=(f"v{tt}",), tag=f"vev{sg}_{tt}")

        if debug and sg == 0:
            DMA("gpsimd", lambda e: e.dma_start(out=dbg["vtm"], in_=v_bf[:]),
                reads=("v0", "v1", "v2", "v3"), tag="dma_dbgv")

        emit_s_tiles(NS_PER_SEG // 2)

        for h in range(HPC):
            hof = h * 132
            for c in range(16):
                E("tensor", lambda e, c=c, h=h: e.matmul(
                    out=ps[0][:], lhsT=wq_sb[:, c * 512 + h * 128: c * 512 + (h + 1) * 128],
                    rhs=xn_fm[:, c * 512:(c + 1) * 512],
                    start=(c == 0), stop=(c == 15)),
                    reads=(f"xn_fm{c}", "wq_sb"), writes=("ps0",), tag=f"qmm{sg}_{h}_{c}")
            for c in range(16):
                E("tensor", lambda e, c=c, h=h: e.matmul(
                    out=ps[1][:], lhsT=wk_sb[:, c * 512 + h * 128: c * 512 + (h + 1) * 128],
                    rhs=xn_fm[:, c * 512:(c + 1) * 512],
                    start=(c == 0), stop=(c == 15)),
                    reads=(f"xn_fm{c}", "wk_sb"), writes=("ps1",), tag=f"kmm{sg}_{h}_{c}")
            E("scalar", lambda e: e.activation(out=q_bf[:], in_=ps[0][:], func=AF.Copy),
              reads=("ps0",), writes=("q_bf",), tag=f"qc{sg}_{h}")
            E("scalar", lambda e: e.activation(out=eq_f[:], in_=ps[0][:], func=AF.Exp),
              reads=("ps0",), writes=("eq",), tag=f"qe{sg}_{h}")
            E("vector", lambda e: e.tensor_single_scalar(out=eq_f[:], in_=eq_f[:],
                                                         scalar=1.0, op=ALU.min),
              reads=("eq",), writes=("eq",), tag=f"qm{sg}_{h}")
            E("vector", lambda e: e.scalar_tensor_tensor(
                out=sq_bf[:], in0=q_bf[:], scalar=0.0, in1=eq_f[:], op0=ALU.max, op1=ALU.add),
                reads=("q_bf", "eq"), writes=("sq_bf",), tag=f"sq{sg}_{h}")
            E("scalar", lambda e: e.activation(out=k_bf[:], in_=ps[1][:], func=AF.Copy),
              reads=("ps1",), writes=("k_bf",), tag=f"kc{sg}_{h}")
            E("scalar", lambda e: e.activation(out=ek_f[:], in_=ps[1][:], func=AF.Exp),
              reads=("ps1",), writes=("ek",), tag=f"ke{sg}_{h}")
            E("vector", lambda e: e.tensor_single_scalar(out=ek_f[:], in_=ek_f[:],
                                                         scalar=1.0, op=ALU.min),
              reads=("ek",), writes=("ek",), tag=f"km{sg}_{h}")
            E("vector", lambda e: e.scalar_tensor_tensor(
                out=sk_bf[:], in0=k_bf[:], scalar=0.0, in1=ek_f[:], op0=ALU.max, op1=ALU.add),
                reads=("k_bf", "ek"), writes=("sk_bf",), tag=f"sk{sg}_{h}")

            if debug and sg == 0 and h == 0:
                DMA("gpsimd", lambda e: e.dma_start(out=dbg["qfm"], in_=q_bf[:]),
                    reads=("q_bf",), tag="dma_dbgq")
                DMA("gpsimd", lambda e: e.dma_start(out=dbg["sq"], in_=sq_bf[:]),
                    reads=("sq_bf",), tag="dma_dbgsq")

            for i in range(4):
                W = (i + 1) * 128
                pscn = "ps2" if i % 2 == 0 else "ps3"
                psc = ps[2 + (i % 2)]
                E("tensor", lambda e, psc=psc, i=i, W=W: e.matmul(
                    out=psc[:, 0:W], lhsT=q_bf[:, i * 128:(i + 1) * 128], rhs=k_bf[:, 0:W],
                    start=True, stop=True),
                    reads=("q_bf", "k_bf"), writes=(pscn,), tag=f"scm{sg}_{h}_{i}")
                E("vector", lambda e, psc=psc, i=i: e.tensor_tensor(
                    out=psc[:, i * 128:(i + 1) * 128], in0=psc[:, i * 128:(i + 1) * 128],
                    in1=dmask[:], op=ALU.add),
                    reads=(pscn, "dmask"), writes=(pscn,), tag=f"msk{sg}_{h}_{i}")
                E("vector", lambda e, psc=psc, W=W: e.reduce_max(
                    out=rmax_c[:], in_=psc[:, 0:W], axis=AX.X),
                    reads=(pscn,), writes=("rmax",), tag=f"rmx{sg}_{h}_{i}")
                E("vector", lambda e: e.tensor_single_scalar(
                    out=ebias_c[:], in_=rmax_c[:], scalar=-INV_SC, op=ALU.mult),
                    reads=("rmax",), writes=("ebias",), tag=f"ebi{sg}_{h}_{i}")
                E("scalar", lambda e, psc=psc, i=i, W=W: e.activation(
                    out=p_ar[:, i * 512: i * 512 + W], in_=psc[:, 0:W], func=AF.Exp,
                    bias=ebias_c[:], scale=INV_SC, accum_out=accs[:, i:i + 1]),
                    reads=(pscn,), sreads=("ebias",), writes=(f"p{i}", f"acc{i}"),
                    tag=f"exp{sg}_{h}_{i}")
            E("vector", lambda e: e.reciprocal(out=rl_f[:], in_=accs[:]),
              reads=("acc0", "acc1", "acc2", "acc3"), writes=("rl_f",), tag=f"rl{sg}_{h}")
            E("vector", lambda e: e.tensor_copy(out=rl_bf[:], in_=rl_f[:]),
              reads=("rl_f",), writes=("rl_bf",), tag=f"rlc{sg}_{h}")
            for i in range(4):
                E("tensor", lambda e, i=i: e.transpose(
                    out=ptb[0][0:1, i * 128:(i + 1) * 128], in_=rl_bf[:, i:i + 1],
                    identity=ident[:]),
                    reads=("rl_bf",), writes=("ptb0",), tag=f"rlT{sg}_{h}_{i}")
            E("vector", lambda e: e.tensor_copy(out=rlrow_bf[:], in_=ptb[0][0:1, 0:512]),
              reads=("ptb0",), writes=("rlrow",), tag=f"rle{sg}_{h}")

            if debug and sg == 0 and h == 0:
                DMA("gpsimd", lambda e: e.dma_start(out=dbg["p0"], in_=p_ar[:]),
                    reads=("p0", "p1", "p2", "p3"), tag="dma_dbgp")

            for j in range(4):
                pb = j % 2
                pbn = f"ptb{pb}"
                for i in range(j, 4):
                    E("tensor", lambda e, i=i, j=j, pb=pb: e.transpose(
                        out=ptb[pb][:, i * 128:(i + 1) * 128],
                        in_=p_ar[:, i * 512 + j * 128: i * 512 + (j + 1) * 128],
                        identity=ident[:]),
                        reads=(f"p{i}",), writes=(pbn,), tag=f"pT{sg}_{h}_{j}_{i}")
                E("vector", lambda e, j=j, pb=pb: e.tensor_copy(
                    out=pTa[:, j * 128:512], in_=ptb[pb][:, j * 128:512]),
                    reads=(pbn,), writes=("pTa",), tag=f"pTe{sg}_{h}_{j}")
                E("tensor", lambda e, j=j, h=h: e.matmul(
                    out=ps[4][:, j * 128:512],
                    lhsT=v_bf[:, j * 512 + h * 128: j * 512 + (h + 1) * 128],
                    rhs=pTa[:, j * 128:512], start=(j == 0), stop=(j == 3)),
                    reads=("pTa", f"v{j}"), writes=("ps4",), tag=f"adm{sg}_{h}_{j}")

            E("scalar", lambda e, hof=hof: e.activation(
                out=memz_bf[:, 0:130], in_=mem_f[:, hof:hof + 130], func=AF.Copy),
                reads=(f"mem{h}", f"mem{h}z"), writes=("memz",), tag=f"mzc{sg}_{h}")
            E("tensor", lambda e: e.matmul(
                out=ps[3][:], lhsT=memz_bf[:, 0:128], rhs=sq_bf[:], start=True, stop=True),
                reads=("memz", "sq_bf"), writes=("ps3",), tag=f"amm{sg}_{h}")
            E("tensor", lambda e: e.matmul(
                out=ps[5][0:1, :], lhsT=memz_bf[:, 128:129], rhs=sq_bf[:], start=True, stop=True),
                reads=("memz", "sq_bf"), writes=("ps5",), tag=f"zrm{sg}_{h}")
            E("vector", lambda e: e.reciprocal(out=rnrow_f[:], in_=ps[5][0:1, :]),
              reads=("ps5",), writes=("rnrow_f",), tag=f"rnr{sg}_{h}")

            for cch in range(4):
                E("tensor", lambda e, cch=cch: e.matmul(
                    out=ps[5][:, 0:130], lhsT=sk_bf[:, cch * 128:(cch + 1) * 128],
                    rhs=memz_bf[:, 0:130], start=True, stop=True),
                    reads=("sk_bf", "memz"), writes=("ps5",), tag=f"kdm{sg}_{h}_{cch}")
                E("vector", lambda e: e.reciprocal(out=rkn_c[:], in_=ps[5][:, 128:129]),
                  reads=("ps5",), writes=("rkn",), tag=f"kdr{sg}_{h}_{cch}")
                E("vector", lambda e, cch=cch, h=h: e.scalar_tensor_tensor(
                    out=delta_bf[:, cch * 128:(cch + 1) * 128], in0=ps[5][:, 0:128],
                    scalar=rkn_c[:], in1=v_bf[:, cch * 512 + h * 128:cch * 512 + (h + 1) * 128],
                    op0=ALU.mult, op1=ALU.subtract),
                    reads=("ps5", f"v{cch}"), sreads=("rkn",), writes=(f"delta{cch}",),
                    tag=f"dlt{sg}_{h}_{cch}")
                pb = cch % 2
                E("tensor", lambda e, cch=cch, pb=pb: e.transpose(
                    out=ptb[pb][:, cch * 128:(cch + 1) * 128],
                    in_=sk_bf[:, cch * 128:(cch + 1) * 128], identity=ident[:]),
                    reads=("sk_bf",), writes=(f"ptb{pb}",), tag=f"skT{sg}_{h}_{cch}")
                E("vector", lambda e, cch=cch, pb=pb: e.tensor_copy(
                    out=sk_tm[:, cch * 128:(cch + 1) * 128],
                    in_=ptb[pb][:, cch * 128:(cch + 1) * 128]),
                    reads=(f"ptb{pb}",), writes=(f"sktm{cch}",), tag=f"skTe{sg}_{h}_{cch}")
            for cch in range(4):
                E("tensor", lambda e, cch=cch: e.matmul(
                    out=ps[5][:, 0:128], lhsT=sk_tm[:, cch * 128:(cch + 1) * 128],
                    rhs=delta_bf[:, cch * 128:(cch + 1) * 128],
                    start=(cch == 0), stop=(cch == 3)),
                    reads=(f"sktm{cch}", f"delta{cch}"), writes=("ps5",), tag=f"mum{sg}_{h}_{cch}")
            E("vector", lambda e: e.reduce_sum(out=zups[:], in_=sk_bf[:], axis=AX.X),
              reads=("sk_bf",), writes=("zups",), tag=f"zsum{sg}_{h}")
            E("vector", lambda e, hof=hof: e.tensor_tensor(
                out=mem_f[:, hof:hof + 128], in0=mem_f[:, hof:hof + 128],
                in1=ps[5][:, 0:128], op=ALU.subtract),
                reads=("ps5",), writes=(f"mem{h}",), tag=f"mup{sg}_{h}")
            E("vector", lambda e, hof=hof: e.tensor_tensor(
                out=mem_f[:, hof + 128:hof + 129], in0=mem_f[:, hof + 128:hof + 129],
                in1=zups[:], op=ALU.add),
                reads=("zups",), writes=(f"mem{h}z",), tag=f"zup{sg}_{h}")

            bcast_row(f"rnb{sg}_{h}", "rnrow_f", rnrow_f[:], 512, "ps0", ps[0][:])
            bcast_row(f"rlb{sg}_{h}", "rlrow", rlrow_bf[:], 512, "ps1", ps[1][:])
            E("vector", lambda e, h=h: e.tensor_scalar(
                out=u_f[:], in0=ps[3][:], scalar1=beta_c[:, h:h + 1], scalar2=None,
                op0=ALU.mult),
                reads=("ps3",), sreads=("beta",), writes=("u_f",), tag=f"ep1{sg}_{h}")
            E("vector", lambda e: e.tensor_tensor(out=t2_f[:], in0=u_f[:], in1=ps[0][:],
                                                  op=ALU.mult),
              reads=("u_f", "ps0"), writes=("t2_f",), tag=f"ep2{sg}_{h}")
            E("vector", lambda e, h=h: e.tensor_scalar(
                out=u_f[:], in0=ps[4][:], scalar1=ombeta_c[:, h:h + 1], scalar2=None,
                op0=ALU.mult),
                reads=("ps4",), sreads=("ombeta",), writes=("u_f",), tag=f"ep3{sg}_{h}")
            E("vector", lambda e: e.tensor_tensor(out=t3_f[:], in0=u_f[:], in1=ps[1][:],
                                                  op=ALU.mult),
              reads=("u_f", "ps1"), writes=("t3_f",), tag=f"ep4{sg}_{h}")
            E("vector", lambda e: e.tensor_tensor(out=att_sb[:], in0=t2_f[:], in1=t3_f[:],
                                                  op=ALU.add),
              reads=("t2_f", "t3_f"), writes=("att_sb",), tag=f"ep5{sg}_{h}")
            DMA("gpsimd", lambda e, sg=sg, h=h: e.dma_start(
                out=attfm_scr[sg * HPC + h], in_=att_sb[:]),
                reads=("att_sb",), writes=(f"afm{sg}_{h}",), tag=f"dma_ast{sg}_{h}")
            if debug:
                DMA("gpsimd", lambda e, sg=sg, h=h: e.dma_start(
                    out=dbg["attfm"][sg * HPC + h], in_=att_sb[:]),
                    reads=("att_sb",), tag=f"dma_dbgast{sg}_{h}")

        emit_s_tiles(NS_PER_SEG - NS_PER_SEG // 2)

    emit_s_tiles(len(s_tiles))

    E("scalar", lambda e: e.activation(out=stmp[:], in_=stmp_ar[:, 191:192], func=AF.Copy),
      reads=("stmp191",), writes=("sguard",), tag="sguard")
    for wi in range(3):
        E("vector", lambda e, wi=wi: e.reduce_sum(
            out=spart[:, wi:wi + 1], in_=stmp_ar[:, wi * 64:(wi + 1) * 64], axis=AX.X),
            reads=("sguard",) + tuple(f"stmp{j}" for j in range(wi * 64, (wi + 1) * 64)),
            writes=("spart",), tag=f"spred{wi}")
    for wi in range(3):
        DMA("gpsimd", lambda e, wi=wi: e.dma_start(
            out=sred_scr[0:1, wi * 128:(wi + 1) * 128].rearrange("one n -> n one"),
            in_=spart[:, wi:wi + 1]),
            reads=("spart",), writes=(f"sred{wi}",), tag=f"dma_sw{wi}")
    DMA("sync", lambda e: e.dma_start(out=srow384[:], in_=sred_scr),
        reads=("sred0", "sred1", "sred2"), writes=("srow384",), tag="dma_sr")
    for wi in range(3):
        E("vector", lambda e, wi=wi: e.reduce_sum(
            out=hl_tmp[0:1, 8 * wi:8 * wi + 8],
            in_=srow384[0:1, wi * 128:(wi + 1) * 128].rearrange("one (a b) -> one a b", a=8),
            axis=AX.X),
            reads=("srow384",), writes=(f"svp{wi}",), tag=f"sredA{wi}")
        E("vector", lambda e, wi=wi: e.reduce_sum(
            out=svrow[0:1, 9 + wi:10 + wi], in_=hl_tmp[0:1, 8 * wi:8 * wi + 8],
            axis=AX.X),
            reads=(f"svp{wi}",), writes=("sv_s",), tag=f"sredB{wi}")
    E("vector", lambda e: e.tensor_single_scalar(
        out=svrow[0:1, 9:12], in_=svrow[0:1, 9:12], scalar=1.0 / NW_ELEM, op=ALU.mult),
        reads=("sv_s",), writes=("sv_s",), tag="snorm")
    E("vector", lambda e: e.tensor_scalar(
        out=svrow[0:1, 0:3], in0=svrow[0:1, 9:12], scalar1=EPS_FFN, scalar2=ATANH_HALF,
        op0=ALU.add, op1=ALU.mult),
        reads=("sv_s",), writes=("sv_th",), tag="sth")
    E("vector", lambda e: e.tensor_single_scalar(
        out=svrow[0:1, 3:6], in_=svrow[0:1, 0:3], scalar=-1.0, op=ALU.mult),
        reads=("sv_th",), writes=("sv_nth",), tag="snth")
    E("scalar", lambda e: e.activation(out=hl_tmp[0:1, 0:3], in_=svrow[0:1, 9:12],
                                       func=AF.Ln, bias=cst[0:1, 1:2], scale=1.0),
      reads=("sv_s",), sreads=("cst",), writes=("ln1p",), tag="sln1p")
    E("scalar", lambda e: e.activation(out=hl_tmp[0:1, 4:7], in_=svrow[0:1, 9:12],
                                       func=AF.Ln, bias=cst[0:1, 1:2], scale=-1.0),
      reads=("sv_s",), sreads=("cst",), writes=("ln1m",), tag="sln1m")
    E("vector", lambda e: e.tensor_tensor(out=svrow[0:1, 6:9], in0=hl_tmp[0:1, 0:3],
                                          in1=hl_tmp[0:1, 4:7], op=ALU.subtract),
      reads=("ln1p", "ln1m"), writes=("sv_a",), tag="sa1")
    E("vector", lambda e: e.tensor_single_scalar(
        out=svrow[0:1, 6:9], in_=svrow[0:1, 6:9], scalar=0.5, op=ALU.mult),
        reads=("sv_a",), writes=("sv_a",), tag="sa2")
    bcast_row("sbc", "sv_a", svrow[0:1, 0:12], 12, "ps5", ps[5][:, 0:12])
    E("vector", lambda e: e.tensor_copy(out=scol[:], in_=ps[5][:, 0:12]),
      reads=("ps5",), writes=("scol",), tag="scole")
    if debug:
        DMA("gpsimd", lambda e: e.dma_start(out=dbg["svals"], in_=scol[:]),
            reads=("scol",), tag="dma_dbgsv")

    nc.sbuf_base = sbuf_base_A

    final_marks = []

    if do_ap:
        P.set_barrier()
        sbuf_base_P = nc.sbuf_base

        def sbp(name, shape, dt):
            return nc.alloc_sbuf_tensor(name, shape, dt)

        wo_sb = sbp("p_wo", [128, 16 * 2048], BF16)
        att4 = sbp("p_att4", [128, 2 * 2048], BF16)
        ostage = sbp("p_ostage", [128, 2 * 2048], F32)

        DMA("gpsimd", lambda e: e.dma_start(
            out=wo_sb.ap().rearrange("p (c n) -> p c n", c=16),
            in_=wo.rearrange("(c p) n -> p c n", p=128)), writes=("wo_sb",), tag="dma_wo")

        for sg in range(nseg):
            db = sg % 2
            a4 = att4[:, db * 2048:(db + 1) * 2048]
            ost = ostage[:, db * 2048:(db + 1) * 2048]
            DMA("sync", lambda e, sg=sg, a4=a4: e.dma_start(
                out=a4.rearrange("p (h n) -> p h n", h=4),
                in_=attfm_scr[sg * HPC:(sg + 1) * HPC].rearrange("h p n -> p h n")),
                reads=tuple(f"afm{sg}_{h}" for h in range(HPC)),
                writes=(f"att4_{db}",), tag=f"dma_a4_{sg}")
            av = a4.rearrange("p (g m j) -> p g m j", g=4, m=32, j=16)
            for j in range(16):
                for fg in range(4):
                    E("tensor", lambda e, j=j, fg=fg, av=av: e.matmul(
                        out=ps[fg][:], lhsT=av[:, :, :, j],
                        rhs=wo_sb[:, j * 2048 + fg * 512: j * 2048 + (fg + 1) * 512],
                        start=(j == 0), stop=(j == 15)),
                        reads=(f"att4_{db}", "wo_sb"), writes=(f"ps{fg}",),
                        tag=f"wom{sg}_{j}_{fg}")
            for fg in range(4):
                E("scalar", lambda e, fg=fg, ost=ost: e.activation(
                    out=ost[:, fg * 512:(fg + 1) * 512], in_=ps[fg][:], func=AF.Copy),
                    reads=(f"ps{fg}",), writes=(f"ost{db}_{fg}",), tag=f"woe{sg}_{fg}")
            mow = DMA("gpsimd", lambda e, sg=sg, ost=ost: e.dma_start(
                out=attown_scr[sg * 128:(sg + 1) * 128], in_=ost),
                reads=tuple(f"ost{db}_{fg}" for fg in range(4)),
                writes=(f"aow{sg}",), tag=f"dma_ao_{sg}")
            if debug:
                DMA("gpsimd", lambda e, sg=sg, ost=ost: e.dma_start(
                    out=dbg["attown"][sg * 128:(sg + 1) * 128], in_=ost),
                    reads=tuple(f"ost{db}_{fg}" for fg in range(4)), tag=f"dma_dbgao_{sg}")
        nc.sbuf_base = sbuf_base_P

    if do_b:
        P.set_barrier()
        g1row = nc.alloc_sbuf_tensor("b_g1row", [1, D], F32)
        g2row = nc.alloc_sbuf_tensor("b_g2row", [1, D], F32)
        g3rsl = nc.alloc_sbuf_tensor("b_g3rsl", [1, 512], F32)
        g3b = nc.alloc_sbuf_tensor("b_g3b", [128, 512], F32)
        rs_ar = nc.alloc_sbuf_tensor("b_rs", [128, 48], F32)
        c3_ar = nc.alloc_sbuf_tensor("b_c3", [128, 16], F32)
        rstd3_ar = nc.alloc_sbuf_tensor("b_rstd3", [128, 16], F32)
        ssq3_ar = nc.alloc_sbuf_tensor("b_ssq3", [128, 16], F32)
        mxy3_ar = nc.alloc_sbuf_tensor("b_mxy3", [128, 16], F32)
        t1c = nc.alloc_sbuf_tensor("b_t1c", [128, 1], F32)
        t2c = nc.alloc_sbuf_tensor("b_t2c", [128, 1], F32)
        t3c = nc.alloc_sbuf_tensor("b_t3c", [128, 1], F32)
        t4c = nc.alloc_sbuf_tensor("b_t4c", [128, 1], F32)
        ln_st_b = nc.alloc_sbuf_tensor("b_lnst", [128, 24], F32)
        ln_ag_b = nc.alloc_sbuf_tensor("b_lnag", [128, 2], F32)

        DMA("sync", lambda e: e.dma_start(out=g1row[:], in_=g1), writes=("g1row",), tag="dma_g1")
        DMA("sync", lambda e: e.dma_start(out=g2row[:], in_=g2), writes=("g2row",), tag="dma_g2")
        E("vector", lambda e: e.memset(ssq3_ar[:], 0.0), writes=("ssq3",), tag="ssq30")
        E("vector", lambda e: e.memset(mxy3_ar[:], 0.0), writes=("mxy3",), tag="mxy30")
        E("vector", lambda e: e.memset(rs_ar[:], 0.0), writes=("rs_init",), tag="rs0")
        E("vector", lambda e: e.memset(c3_ar[:], 0.0), writes=("m3",), tag="c30")
        E("vector", lambda e: e.memset(rstd3_ar[:], 0.0), writes=("ms3",), tag="rstd30")

        base_B = nc.sbuf_base
        n1fm = nc.alloc_sbuf_tensor("b_n1fm", [128, 16 * 1024], BF16)
        n2fm = nc.alloc_sbuf_tensor("b_n2fm", [128, 16 * 1024], BF16)
        nc.sbuf_base = base_B
        o3acc = nc.alloc_sbuf_tensor("b_o3acc", [128, 8 * D], F32)
        base_stage = nc.sbuf_base

        xn_own = nc.alloc_sbuf_tensor("bp_xn", [128, D], F32)
        y_f = nc.alloc_sbuf_tensor("bp_y", [128, D], F32)
        tq_f = nc.alloc_sbuf_tensor("bp_t", [128, 1024], F32)
        ti_q = nc.alloc_sbuf_tensor("bp_ti", [128, 1024], I32)
        tf_q = nc.alloc_sbuf_tensor("bp_tf", [128, 1024], F32)
        w01_q = nc.alloc_sbuf_tensor("bp_w01", [128, 1024], F32)
        n_bf = nc.alloc_sbuf_tensor("bp_nbf", [128, D], BF16)
        g1b = nc.alloc_sbuf_tensor("bp_g1b", [128, D], F32)
        g2b = nc.alloc_sbuf_tensor("bp_g2b", [128, D], F32)
        prep_end = nc.sbuf_base
        nc.sbuf_base = base_stage
        T1t = nc.alloc_sbuf_tensor("bs_T1t", [128, 16 * 512], BF16)
        T2t = nc.alloc_sbuf_tensor("bs_T2t", [128, 16 * 512], BF16)
        wstg_b = nc.alloc_sbuf_tensor("bs_wstg", [128, 4 * 512], F32)
        sgA = nc.alloc_sbuf_tensor("bs_sgA", [128, 4 * 512], BF16)
        sgB = nc.alloc_sbuf_tensor("bs_sgB", [128, 4 * 512], BF16)
        sw_f = nc.alloc_sbuf_tensor("bs_sw", [128, 512], F32)
        xv_f = nc.alloc_sbuf_tensor("bs_xv", [128, 512], F32)
        h_f = nc.alloc_sbuf_tensor("bs_h", [128, 512], F32)
        y3_f = nc.alloc_sbuf_tensor("bs_y3", [128, 2 * 512], F32)
        sqd_f = nc.alloc_sbuf_tensor("bs_sqd", [128, 512], F32)
        slab_end = nc.sbuf_base
        nc.sbuf_base = base_stage
        T3t = nc.alloc_sbuf_tensor("b2_T3t", [128, 8 * D], BF16)
        w3stg = nc.alloc_sbuf_tensor("b2_wstg", [128, D], F32)
        sg3A = nc.alloc_sbuf_tensor("b2_sgA", [128, D], BF16)
        sg3B = nc.alloc_sbuf_tensor("b2_sgB", [128, D], BF16)
        y3t = nc.alloc_sbuf_tensor("b2_y3t", [128, 2 * 1024], F32)
        t3q_f = nc.alloc_sbuf_tensor("b2_t3", [128, 1024], F32)
        ti3 = nc.alloc_sbuf_tensor("b2_ti3", [128, 1024], I32)
        tf3 = nc.alloc_sbuf_tensor("b2_tf3", [128, 1024], F32)
        w013 = nc.alloc_sbuf_tensor("b2_w013", [128, 1024], F32)
        n3_bf = nc.alloc_sbuf_tensor("b2_n3", [128, 1024], BF16)
        n3fm = nc.alloc_sbuf_tensor("b2_n3fm", [128, 8 * 128], BF16)
        b2_end = nc.sbuf_base
        nc.sbuf_base = base_stage
        xo_t = nc.alloc_sbuf_tensor("bf_xo", [128, 2 * D], F32)
        ao_t = nc.alloc_sbuf_tensor("bf_ao", [128, 2 * D], F32)
        os_t = nc.alloc_sbuf_tensor("bf_os", [128, 2 * D], F32)
        nc.sbuf_base = max(prep_end, slab_end, b2_end)

        def quant_chain(src_ap, src_buf, cc_ap, cc_buf, dst_ap, dst_buf, width,
                        tbufs, tag):
            t_a, ti_a, tf_a, w_a = tbufs
            E("vector", lambda e: e.tensor_scalar(
                out=t_a[:, 0:width], in0=src_ap, scalar1=cc_ap, scalar2=128.5,
                op0=ALU.mult, op1=ALU.add),
                reads=(src_buf,), sreads=(cc_buf,), writes=(tag + "_t",), tag=tag + "_t")
            E("vector", lambda e: e.tensor_copy(out=ti_a[:, 0:width], in_=t_a[:, 0:width]),
              reads=(tag + "_t",), writes=(tag + "_ti",), tag=tag + "_ti")
            E("vector", lambda e: e.tensor_copy(out=tf_a[:, 0:width], in_=ti_a[:, 0:width]),
              reads=(tag + "_ti",), writes=(tag + "_tf",), tag=tag + "_tf")
            E("vector", lambda e: e.tensor_tensor(
                out=w_a[:, 0:width], in0=tf_a[:, 0:width], in1=t_a[:, 0:width], op=ALU.is_gt),
                reads=(tag + "_tf", tag + "_t"), writes=(tag + "_w",), tag=tag + "_w")
            E("vector", lambda e: e.scalar_tensor_tensor(
                out=dst_ap, in0=tf_a[:, 0:width], scalar=-128.0, in1=w_a[:, 0:width],
                op0=ALU.add, op1=ALU.subtract),
                reads=(tag + "_tf", tag + "_w"), writes=(dst_buf,), tag=tag + "_n")

        def ternary_tiles(wix, stg, stg_buf, stg_view, a_flat, b_flat, dst_ap, dst_buf,
                          src_sel, width, tag):
            DMA("scalar", lambda e: e.dma_start(out=stg_view, in_=src_sel),
                writes=(stg_buf,), tag=tag + "_ld")
            E("scalar", lambda e: e.activation(
                out=a_flat, in_=stg.ap()[:, 0:width], func=AF.Sign, bias=scol[:, 3 + wix:4 + wix]),
                reads=(stg_buf,), sreads=("scol",), writes=("sgA",), tag=tag + "_a")
            E("scalar", lambda e: e.activation(
                out=b_flat, in_=stg.ap()[:, 0:width], func=AF.Sign, bias=scol[:, 0 + wix:1 + wix]),
                reads=(stg_buf,), sreads=("scol",), writes=("sgB",), tag=tag + "_b")
            E("vector", lambda e: e.tensor_tensor(out=dst_ap, in0=a_flat, in1=b_flat, op=ALU.add),
              reads=("sgA", "sgB"), writes=(dst_buf,), tag=tag + "_s")

        for tg in range(2):
            P.set_barrier()
            for gi, (grow_ap, grow_buf, gb, gbn) in enumerate(
                    ((g1row, "g1row", g1b, "g1b"), (g2row, "g2row", g2b, "g2b"))):
                for q in range(4):
                    bcast_row(f"gb{tg}_{gi}_{q}", grow_buf, grow_ap[0:1, q * 512:(q + 1) * 512],
                              512, "ps4", ps[4][:])
                    E("vector", lambda e, gb=gb, q=q: e.tensor_copy(
                        out=gb[:, q * 512:(q + 1) * 512], in_=ps[4][:]),
                        reads=("ps4",), writes=(gbn + str(q),), tag=f"gbe{tg}_{gi}_{q}")
            for tt in range(8):
                gti = tg * 8 + tt
                DMA("sync", lambda e, gti=gti: e.dma_start(
                    out=xn_own[:], in_=xo_in[gti * 128:(gti + 1) * 128, :]),
                    writes=("xn_own_raw", "xn_own"), tag=f"dma_xo{gti}")
                for kq in range(4):
                    E("vector", lambda e, kq=kq: e.bn_stats(
                        out=ln_st_b[:, kq * 6:(kq + 1) * 6],
                        in_=xn_own[:, kq * 512:(kq + 1) * 512]),
                        reads=("xn_own_raw",), writes=("lnstb",), tag=f"bnb{gti}_{kq}")
                E("vector", lambda e: e.bn_aggr(out=ln_ag_b[:], in_=ln_st_b[:]),
                  reads=("lnstb",), writes=("lnagb",), tag=f"agb{gti}")
                E("scalar", lambda e: e.activation(out=t1c[:], in_=ln_ag_b[:, 1:2],
                                                   func=AF.Sqrt, bias=cst[:, 0:1]),
                  reads=("lnagb",), sreads=("cst",), writes=("t1c",), tag=f"sdb{gti}")
                E("vector", lambda e: e.reciprocal(out=t2c[:], in_=t1c[:]),
                  reads=("t1c",), writes=("rstdb",), tag=f"rstdb{gti}")
                E("vector", lambda e: e.scalar_tensor_tensor(
                    out=t3c[:], in0=ln_ag_b[:, 0:1], scalar=-1.0, in1=t2c[:],
                    op0=ALU.mult, op1=ALU.mult),
                    reads=("lnagb", "rstdb"), writes=("lnbb",), tag=f"lnbb{gti}")
                E("vector", lambda e: e.tensor_scalar(
                    out=xn_own[:], in0=xn_own[:], scalar1=t2c[:], scalar2=t3c[:],
                    op0=ALU.mult, op1=ALU.add),
                    reads=("xn_own_raw",), sreads=("lnbb", "rstdb"), writes=("xn_own",),
                    tag=f"lnab{gti}")
                for br in range(2):
                    gb, gbn = (g1b, "g1b") if br == 0 else (g2b, "g2b")
                    nfm = n1fm if br == 0 else n2fm
                    nfmn = f"n{br+1}fm_t{tt}"
                    E("vector", lambda e, gb=gb: e.tensor_tensor(
                        out=y_f[:], in0=xn_own[:], in1=gb[:], op=ALU.mult),
                        reads=("xn_own",) + tuple(gbn + str(q) for q in range(4)),
                        writes=("y_f",), tag=f"yb{gti}_{br}")
                    E("vector", lambda e: e.reduce_max(out=t1c[:], in_=y_f[:], axis=AX.X,
                                                       apply_absolute_value=True),
                      reads=("y_f",), writes=("mraw",), tag=f"mr{gti}_{br}")
                    E("vector", lambda e: e.tensor_tensor(out=t3c[:], in0=t1c[:], in1=t2c[:],
                                                          op=ALU.mult),
                      reads=("mraw", "rstdb"), writes=("m1",), tag=f"m1{gti}_{br}")
                    E("vector", lambda e: e.tensor_single_scalar(
                        out=t3c[:], in_=t3c[:], scalar=1e-4, op=ALU.max),
                        reads=("m1",), writes=("m1c",), tag=f"m1c{gti}_{br}")
                    E("vector", lambda e: e.reciprocal(out=t4c[:], in_=t3c[:]),
                      reads=("m1c",), writes=("invm",), tag=f"inv{gti}_{br}")
                    E("vector", lambda e: e.scalar_tensor_tensor(
                        out=t4c[:], in0=t4c[:], scalar=127.0, in1=t2c[:],
                        op0=ALU.mult, op1=ALU.mult),
                        reads=("invm", "rstdb"), writes=("c1",), tag=f"c1{gti}_{br}")
                    E("vector", lambda e, br=br, gti=gti: e.scalar_tensor_tensor(
                        out=rs_ar[:, br * 16 + gti: br * 16 + gti + 1], in0=t3c[:],
                        scalar=1.0 / 254.0, in1=scol[:, 6 + br:7 + br],
                        op0=ALU.mult, op1=ALU.mult),
                        reads=("m1c", "scol"), writes=(f"rs{br}_{gti}",), tag=f"rs{gti}_{br}")
                    for hf in range(2):
                        quant_chain(y_f[:, hf * 1024:(hf + 1) * 1024], "y_f",
                                    t4c[:], "c1",
                                    n_bf[:, hf * 1024:(hf + 1) * 1024], f"n_bf{hf}",
                                    1024, (tq_f, ti_q, tf_q, w01_q), f"q{gti}_{br}_{hf}")
                    if debug and gti == 0 and br == 1:
                        DMA("gpsimd", lambda e: e.dma_start(out=dbg["n1"], in_=n_bf[:]),
                            reads=("n_bf0", "n_bf1"), tag="dma_dbgn1")

                    for cb in range(4):
                        pb = cb % 2
                        pbn = f"ptb{pb}"
                        for jj in range(4):
                            c = cb * 4 + jj
                            E("tensor", lambda e, c=c, jj=jj, pb=pb: e.transpose(
                                out=ptb[pb][:, jj * 128:(jj + 1) * 128],
                                in_=n_bf[:, c * 128:(c + 1) * 128], identity=ident[:]),
                                reads=(f"n_bf{c // 8}", "ident"), writes=(pbn,),
                                tag=f"nT{gti}_{br}_{c}")
                        E("vector", lambda e, cb=cb, pb=pb, nfm=nfm, tt=tt: e.tensor_copy(
                            out=nfm.ap().rearrange("p (c n) -> p c n", c=16)[
                                :, cb * 4:(cb + 1) * 4, tt * 128:(tt + 1) * 128],
                            in_=ptb[pb][:].rearrange("p (j n) -> p j n", j=4)),
                            reads=(pbn,), writes=(nfmn + f"_{cb}",), tag=f"nTe{gti}_{br}_{cb}")

            P.set_barrier()
            for sl in range(16):
                for wix, (wt_src, Tt, Ttn) in enumerate(((w1t, T1t, "T1t"), (w2t, T2t, "T2t"))):
                    wsrcv = wt_src.rearrange("(c p) o -> p c o", p=128)
                    for hf in range(4):
                        ternary_tiles(
                            wix, wstg_b, "wstg_b",
                            wstg_b.ap().rearrange("p (c n) -> p c n", c=4),
                            sgA[:, 0:4 * 512], sgB[:, 0:4 * 512],
                            Tt[:, hf * 4 * 512:(hf + 1) * 4 * 512], f"{Ttn}_{hf}",
                            wsrcv[:, hf * 4:(hf + 1) * 4, sl * 512:(sl + 1) * 512],
                            4 * 512, f"tern{tg}_{sl}_{wix}_{hf}")
                DMA("sync", lambda e, sl=sl: e.dma_start(
                    out=g3rsl[:], in_=g3[0:1, sl * 512:(sl + 1) * 512]),
                    writes=("g3rsl",), tag=f"dma_g3r{tg}_{sl}")
                bcast_row(f"g3b{tg}_{sl}", "g3rsl", g3rsl[:], 512, "ps4", ps[4][:])
                E("vector", lambda e: e.tensor_copy(out=g3b[:], in_=ps[4][:]),
                  reads=("ps4",), writes=("g3b",), tag=f"g3be{tg}_{sl}")
                if debug and tg == 0 and sl == 0:
                    DMA("gpsimd", lambda e: e.dma_start(out=dbg["t1t"], in_=T2t[:, 0:512]),
                        reads=("T2t_0",), tag="dma_dbgt1t")
                for tt in range(8):
                    gti = tg * 8 + tt
                    pmA = ps[0 if tt % 2 == 0 else 2]
                    pmB = ps[1 if tt % 2 == 0 else 3]
                    pmAn = "ps0" if tt % 2 == 0 else "ps2"
                    pmBn = "ps1" if tt % 2 == 0 else "ps3"
                    for c in range(16):
                        E("tensor", lambda e, c=c, tt=tt, pmA=pmA: e.matmul(
                            out=pmA[:], lhsT=n1fm[:, c * 1024 + tt * 128: c * 1024 + (tt + 1) * 128],
                            rhs=T1t[:, c * 512:(c + 1) * 512],
                            start=(c == 0), stop=(c == 15)),
                            reads=("T1t_0", "T1t_1", "T1t_2", "T1t_3"), writes=(pmAn,),
                            tag=f"m1_{tg}_{sl}_{tt}_{c}")
                    for c in range(16):
                        E("tensor", lambda e, c=c, tt=tt, pmB=pmB: e.matmul(
                            out=pmB[:], lhsT=n2fm[:, c * 1024 + tt * 128: c * 1024 + (tt + 1) * 128],
                            rhs=T2t[:, c * 512:(c + 1) * 512],
                            start=(c == 0), stop=(c == 15)),
                            reads=("T2t_0", "T2t_1", "T2t_2", "T2t_3"), writes=(pmBn,),
                            tag=f"m2_{tg}_{sl}_{tt}_{c}")
                    E("scalar", lambda e, pmA=pmA, gti=gti: e.activation(
                        out=sw_f[:], in_=pmA[:], func=AF.Silu,
                        scale=rs_ar[:, gti:gti + 1]),
                        reads=(pmAn,), sreads=(f"rs0_{gti}",), writes=("sw_f",),
                        tag=f"sw{tg}_{sl}_{tt}")
                    E("scalar", lambda e, pmB=pmB, gti=gti: e.activation(
                        out=xv_f[:], in_=pmB[:], func=AF.Copy,
                        scale=rs_ar[:, 16 + gti:16 + gti + 1]),
                        reads=(pmBn,), sreads=(f"rs1_{gti}",), writes=("xv_f",),
                        tag=f"xv{tg}_{sl}_{tt}")
                    if debug and tg == 0 and sl == 0 and tt == 0:
                        DMA("gpsimd", lambda e: e.dma_start(out=dbg["swx"][:, 0:512], in_=sw_f[:]),
                            reads=("sw_f",), tag="dma_dbgsw")
                        DMA("gpsimd", lambda e: e.dma_start(out=dbg["swx"][:, 512:1024], in_=xv_f[:]),
                            reads=("xv_f",), tag="dma_dbgxv")
                    E("vector", lambda e: e.tensor_tensor(out=h_f[:], in0=sw_f[:],
                                                          in1=xv_f[:], op=ALU.mult),
                      reads=("sw_f", "xv_f"), writes=("h_f",), tag=f"h{tg}_{sl}_{tt}")
                    yb = y3_f[:, (tt % 2) * 512:(tt % 2 + 1) * 512]
                    ybn = f"y3_{tt % 2}"
                    E("gpsimd", lambda e, yb=yb: e.tensor_tensor(
                        out=yb, in0=h_f[:], in1=g3b[:], op=ALU.mult),
                        reads=("h_f", "g3b"), writes=(ybn,), k=1, tag=f"y3{tg}_{sl}_{tt}")
                    E("scalar", lambda e: e.activation(out=sqd_f[:], in_=h_f[:],
                                                       func=AF.Square, accum_out=t1c[:]),
                      reads=("h_f",), writes=("sqd", "sqtmp"), tag=f"sq{tg}_{sl}_{tt}")
                    E("vector", lambda e, gti=gti: e.tensor_tensor(
                        out=ssq3_ar[:, gti:gti + 1], in0=ssq3_ar[:, gti:gti + 1],
                        in1=t1c[:], op=ALU.add),
                        reads=("sqtmp", "ssq3"), writes=("ssq3",), tag=f"sqa{tg}_{sl}_{tt}")
                    E("vector", lambda e, yb=yb: e.reduce_max(
                        out=t3c[:], in_=yb, axis=AX.X, apply_absolute_value=True),
                        reads=(ybn,), writes=("mxtmp",), tag=f"mx{tg}_{sl}_{tt}")
                    E("vector", lambda e, gti=gti: e.tensor_tensor(
                        out=mxy3_ar[:, gti:gti + 1], in0=mxy3_ar[:, gti:gti + 1],
                        in1=t3c[:], op=ALU.max),
                        reads=("mxtmp", "mxy3"), writes=("mxy3",), tag=f"mxa{tg}_{sl}_{tt}")
                    DMA("gpsimd", lambda e, gti=gti, sl=sl, yb=yb: e.dma_start(
                        out=h_scr[gti * 128:(gti + 1) * 128, sl * 512:(sl + 1) * 512],
                        in_=yb),
                        reads=(ybn,), writes=(f"hscr{gti}_{sl}",), tag=f"dma_h{tg}_{sl}_{tt}")

            if debug and tg == 0:
                DMA("gpsimd", lambda e: e.dma_start(out=dbg["hrow"], in_=h_scr[0:128, :]),
                    reads=tuple(f"hscr0_{sl}" for sl in range(16)), tag="dma_dbghrow")
            P.set_barrier()
            c0, c1_ = tg * 8, tg * 8 + 8
            E("vector", lambda e, c0=c0, c1_=c1_: e.tensor_single_scalar(
                out=rstd3_ar[:, c0:c1_], in_=ssq3_ar[:, c0:c1_], scalar=1.0 / HID,
                op=ALU.mult),
                reads=("ssq3",), writes=("ms3", "sd3", "rstd3"), tag=f"ms3_{tg}")
            E("scalar", lambda e, c0=c0, c1_=c1_: e.activation(out=rstd3_ar[:, c0:c1_],
                                               in_=rstd3_ar[:, c0:c1_],
                                               func=AF.Sqrt, bias=cst[:, 2:3]),
              reads=("ms3",), sreads=("cst",), writes=("sd3",), tag=f"sd3_{tg}")
            E("vector", lambda e, c0=c0, c1_=c1_: e.reciprocal(out=rstd3_ar[:, c0:c1_],
                                               in_=rstd3_ar[:, c0:c1_]),
              reads=("sd3",), writes=("rstd3",), tag=f"rstd3_{tg}")
            E("vector", lambda e, c0=c0, c1_=c1_: e.tensor_tensor(
                out=c3_ar[:, c0:c1_], in0=mxy3_ar[:, c0:c1_], in1=rstd3_ar[:, c0:c1_],
                op=ALU.mult),
                reads=("mxy3", "rstd3"), writes=("m3", "m3c", "inv3", "c3"), tag=f"m3_{tg}")
            E("vector", lambda e, c0=c0, c1_=c1_: e.tensor_single_scalar(
                out=c3_ar[:, c0:c1_], in_=c3_ar[:, c0:c1_], scalar=1e-4, op=ALU.max),
                reads=("m3",), writes=("m3c",), tag=f"m3c_{tg}")
            for tt in range(8):
                gti = tg * 8 + tt
                E("vector", lambda e, gti=gti: e.scalar_tensor_tensor(
                    out=rs_ar[:, 32 + gti:33 + gti], in0=c3_ar[:, gti:gti + 1],
                    scalar=1.0 / 254.0, in1=scol[:, 8:9], op0=ALU.mult, op1=ALU.mult),
                    reads=("m3c", "scol"), writes=(f"rs2_{gti}",), tag=f"rs3_{gti}")
            E("vector", lambda e, c0=c0, c1_=c1_: e.reciprocal(out=c3_ar[:, c0:c1_], in_=c3_ar[:, c0:c1_]),
              reads=("m3c",), writes=("inv3",), tag=f"inv3_{tg}")
            E("vector", lambda e, c0=c0, c1_=c1_: e.scalar_tensor_tensor(
                out=c3_ar[:, c0:c1_], in0=c3_ar[:, c0:c1_], scalar=127.0,
                in1=rstd3_ar[:, c0:c1_], op0=ALU.mult, op1=ALU.mult),
                reads=("inv3", "rstd3"), writes=("c3",), tag=f"c3_{tg}")

            for sl in range(8):
                w3v = w3t.rearrange("(c p) n -> p c n", p=128)
                for qq in range(8):
                    ternary_tiles(
                        2, w3stg, "w3stg",
                        w3stg.ap()[:, :],
                        sg3A[:, 0:D], sg3B[:, 0:D],
                        T3t[:, qq * D:(qq + 1) * D], f"T3t_{qq % 4}",
                        w3v[:, sl * 8 + qq, :],
                        D, f"t3n{tg}_{sl}_{qq}")
                for tt in range(8):
                    gti = tg * 8 + tt
                    yb = y3t[:, (tt % 2) * 1024:(tt % 2 + 1) * 1024]
                    ybn = f"y3t_{tt % 2}"
                    DMA("sync", lambda e, gti=gti, sl=sl, yb=yb: e.dma_start(
                        out=yb, in_=h_scr[gti * 128:(gti + 1) * 128,
                                          sl * 1024:(sl + 1) * 1024]),
                        reads=(f"hscr{gti}_{2 * sl}", f"hscr{gti}_{2 * sl + 1}"),
                        writes=(ybn,), tag=f"dma_hr{tg}_{sl}_{tt}")
                    quant_chain(yb, ybn, c3_ar[:, gti:gti + 1], "c3",
                                n3_bf[:], "n3_bf", 1024,
                                (t3q_f, ti3, tf3, w013), f"q3{tg}_{sl}_{tt}")
                    if debug and tg == 0 and sl == 0 and tt == 0:
                        DMA("gpsimd", lambda e: e.dma_start(
                            out=dbg["probe"][:, 0:1024], in_=n3_bf[:]),
                            reads=("n3_bf",), tag="dma_dbgn3")
                        DMA("gpsimd", lambda e: e.dma_start(
                            out=dbg["probe"][:, 1024:1040], in_=c3_ar[:]),
                            reads=("c3",), tag="dma_dbgc3")
                        DMA("gpsimd", lambda e: e.dma_start(
                            out=dbg["probe"][:, 1040:1056], in_=rstd3_ar[:]),
                            reads=("rstd3",), tag="dma_dbgrstd3")
                        DMA("gpsimd", lambda e: e.dma_start(
                            out=dbg["probe"][:, 1056:1072], in_=ssq3_ar[:]),
                            reads=("ssq3",), tag="dma_dbgssq3")
                        DMA("gpsimd", lambda e: e.dma_start(
                            out=dbg["probe"][:, 1072:2096], in_=yb),
                            reads=(ybn,), tag="dma_dbgyb")
                    for cb in range(2):
                        pb = cb % 2
                        pbn = f"ptb{pb}"
                        for jj in range(4):
                            c = cb * 4 + jj
                            E("tensor", lambda e, c=c, jj=jj, pb=pb: e.transpose(
                                out=ptb[pb][:, jj * 128:(jj + 1) * 128],
                                in_=n3_bf[:, c * 128:(c + 1) * 128], identity=ident[:]),
                                reads=("n3_bf", "ident"), writes=(pbn,),
                                tag=f"n3T{tg}_{sl}_{tt}_{c}")
                        E("vector", lambda e, cb=cb, pb=pb: e.tensor_copy(
                            out=n3fm[:, cb * 512:(cb + 1) * 512], in_=ptb[pb][:]),
                            reads=(pbn,), writes=(f"n3fm_{cb}",), tag=f"n3Te{tg}_{sl}_{tt}_{cb}")
                    for q in range(4):
                        for c in range(8):
                            E("tensor", lambda e, q=q, c=c: e.matmul(
                                out=ps[q][:], lhsT=n3fm[:, c * 128:(c + 1) * 128],
                                rhs=T3t[:, c * D + q * 512: c * D + (q + 1) * 512],
                                start=(c == 0), stop=(c == 7)),
                                reads=(f"n3fm_{c // 4}",) + tuple(f"T3t_{qq}" for qq in range(4)),
                                writes=(f"ps{q}",), tag=f"m3_{tg}_{sl}_{tt}_{q}_{c}")
                    for q in range(4):
                        if sl == 0:
                            E("vector", lambda e, q=q, tt=tt: e.tensor_copy(
                                out=o3acc[:, tt * D + q * 512: tt * D + (q + 1) * 512],
                                in_=ps[q][:]),
                                reads=(f"ps{q}",), writes=(f"o3_{tt}_{q}",),
                                tag=f"o3c{tg}_{sl}_{tt}_{q}")
                        else:
                            E("vector", lambda e, q=q, tt=tt: e.tensor_tensor(
                                out=o3acc[:, tt * D + q * 512: tt * D + (q + 1) * 512],
                                in0=o3acc[:, tt * D + q * 512: tt * D + (q + 1) * 512],
                                in1=ps[q][:], op=ALU.add),
                                reads=(f"ps{q}", f"o3_{tt}_{q}"), writes=(f"o3_{tt}_{q}",),
                                tag=f"o3a{tg}_{sl}_{tt}_{q}")

            if debug and tg == 0:
                DMA("gpsimd", lambda e: e.dma_start(out=dbg["o3"], in_=o3acc[:, 0:D]),
                    reads=tuple(f"o3_0_{q}" for q in range(4)), tag="dma_dbgo3")
            P.set_barrier()
            for tt in range(8):
                gti = tg * 8 + tt
                db = tt % 2
                xo = xo_t[:, db * D:(db + 1) * D]
                ao = ao_t[:, db * D:(db + 1) * D]
                ost = os_t[:, db * D:(db + 1) * D]
                DMA("sync", lambda e, gti=gti, xo=xo: e.dma_start(
                    out=xo, in_=xo_in[gti * 128:(gti + 1) * 128, :]),
                    writes=(f"fxo{db}",), tag=f"dma_fx{gti}")
                DMA("gpsimd", lambda e, gti=gti, ao=ao: e.dma_start(
                    out=ao, in_=attown_scr[gti * 128:(gti + 1) * 128, :]),
                    reads=(f"aow{gti}",) if gti < nseg else (),
                    writes=(f"fao{db}",), tag=f"dma_fa{gti}")
                E("vector", lambda e, tt=tt, xo=xo, ost=ost, gti=gti: e.scalar_tensor_tensor(
                    out=ost, in0=o3acc[:, tt * D:(tt + 1) * D],
                    scalar=rs_ar[:, 32 + gti:33 + gti], in1=xo,
                    op0=ALU.mult, op1=ALU.add),
                    reads=(f"fxo{db}",) + tuple(f"o3_{tt}_{q}" for q in range(4)),
                    sreads=(f"rs2_{gti}",),
                    writes=(f"fs1_{db}",), tag=f"f1_{gti}")
                E("vector", lambda e, ao=ao, ost=ost: e.tensor_tensor(
                    out=ost, in0=ost, in1=ao, op=ALU.add),
                    reads=(f"fao{db}", f"fs1_{db}"), writes=(f"fs2_{db}",), tag=f"f2_{gti}")
                DMA("gpsimd", lambda e, gti=gti, ost=ost: e.dma_start(
                    out=out[gti * 128:(gti + 1) * 128, :], in_=ost),
                    reads=(f"fs2_{db}",), writes=(f"outw{gti}",), tag=f"dma_out{gti}")

    import os as _os
    P.truncate(int(_os.environ.get("MAXOPS", "0")) or None)
    P.finalize()
    blk = ctx.enter_context(nc.Block())
    P.build_streams(blk, sems)
    ctx.close()
    return nc, dbg


def build_phase_b(nc, P, E, DMA, env):
    raise NotImplementedError


import ml_dtypes
from concourse.bass_utils import run_bass_kernel_spmd

_IDENT = np.eye(128, dtype=ml_dtypes.bfloat16)
_DMASK = np.where(np.tril(np.ones((128, 128), bool)), 0.0, -1.0e30).astype(np.float32)
_MEMINIT = np.zeros((128, 130), np.float32)
_MEMINIT[:, 128] = 1.0 / 128.0
_CST = np.tile(np.array([[EPS_LN, 1.0, EPS_FFN]], np.float32), (128, 1))


def kernel(L0_attn_x, L0_attn_Wq, L0_attn_Wk, L0_attn_Wv, L0_attn_Wo, L0_attn_betas,
           L0_ffn_w1, L0_ffn_g1, L0_ffn_w2, L0_ffn_g2, L0_ffn_w3, L0_ffn_g3):
    f = np.float32
    ax = lambda a: np.ascontiguousarray(np.asarray(a, dtype=f))
    xfull = ax(L0_attn_x)
    Wq, Wk, Wv, Wo = ax(L0_attn_Wq), ax(L0_attn_Wk), ax(L0_attn_Wv), ax(L0_attn_Wo)
    betas = ax(L0_attn_betas)
    w1t = ax(np.asarray(L0_ffn_w1).T)
    w2t = ax(np.asarray(L0_ffn_w2).T)
    w3t = ax(np.asarray(L0_ffn_w3).T)
    g1 = ax(L0_ffn_g1)[None, :]
    g2 = ax(L0_ffn_g2)[None, :]
    g3 = ax(L0_ffn_g3)[None, :]

    nc, _dbg = build_core_program(0, debug=0)

    in_maps = []
    for c in range(8):
        b, cp = c // 4, c % 4
        h0 = 4 * cp
        xb = np.ascontiguousarray(xfull[b])
        xo = np.ascontiguousarray(
            xb.reshape(16, 4, 128, D)[:, cp].reshape(16 * 128, D))
        in_maps.append({
            "x": xb, "xo": xo,
            "wq": np.ascontiguousarray(Wq[:, h0 * 128:(h0 + 4) * 128]),
            "wk": np.ascontiguousarray(Wk[:, h0 * 128:(h0 + 4) * 128]),
            "wv": np.ascontiguousarray(Wv[:, h0 * 128:(h0 + 4) * 128]),
            "wo": Wo,
            "betasT": np.ascontiguousarray(betas[0, h0:h0 + 4, 0, :].T),
            "w1t": w1t, "w2t": w2t, "w3t": w3t,
            "g1": g1, "g2": g2, "g3": g3,
            "ident": _IDENT, "dmask": _DMASK, "meminit": _MEMINIT, "cst": _CST,
        })
    res = run_bass_kernel_spmd(nc, in_maps, core_ids=list(range(8)))
    out = np.empty((2, S, D), f)
    for c in range(8):
        b, cp = c // 4, c % 4
        rr = np.asarray(res.results[c]["out"], dtype=f)
        out[b].reshape(16, 4, 128, D)[:, cp] = rr.reshape(16, 128, D)
    return out
